# version 11
# baseline (speedup 1.0000x reference)
import math
from contextlib import ExitStack

import numpy as np
import ml_dtypes

import concourse.bass as bass
import concourse.mybir as mybir
from concourse.bass_utils import run_bass_kernel_spmd

F32 = mybir.dt.float32
BF16 = mybir.dt.bfloat16
AF = mybir.ActivationFunctionType
ALU = mybir.AluOpType
bf16 = ml_dtypes.bfloat16

NCORES = 8
TOK = 4096
TT = 512
NTILE = TOK // TT
DC = 8
SEQ = 8192
EPS = 1e-5

STAGE = 4
TAP_KV = False


class Ev:
    __slots__ = ("sem", "val", "eng")

    def __init__(self, sem, val, eng):
        self.sem, self.val, self.eng = sem, val, eng


class Res:
    __slots__ = ("w", "r", "sem")

    def __init__(self, sem=None):
        self.w = None
        self.r = []
        self.sem = sem


class Prog:
    ENGS = ("pe", "act", "dve", "pool", "sp")

    def __init__(self, nc, es):
        self.nc = nc
        self.q = {e: [] for e in self.ENGS}
        self.sems = {}
        self.cnt = {}
        self.guard = {e: {} for e in self.ENGS}
        self.es = es
        self._nres = 0
        for e in self.ENGS:
            self.new_sem("s_" + e)
        for d in ("d_pre", "d_xb", "d_gate", "d_qo", "d_lq", "d_cc", "d_out"):
            self.new_sem(d)

    def res_sem(self, res):
        if res.sem is None:
            res.sem = f"rs{self._nres}"
            self._nres += 1
            self.new_sem(res.sem)
        return res.sem

    def new_sem(self, name):
        self.sems[name] = self.es.enter_context(self.nc.semaphore(name))
        self.cnt[name] = 0

    def grp(self, eng, fns, reads=(), writes=(), sem=None, inc=1, asyn=False):
        if sem is None:
            sem = self.res_sem(writes[0]) if asyn else ("s_" + eng)
        waits = {}

        def need(ev):
            if ev is None:
                return
            if ev.eng is not None and ev.eng == eng:
                return
            v = self.cnt[ev.sem] if ev.eng is None else ev.val
            if v > waits.get(ev.sem, 0):
                waits[ev.sem] = v

        for r in reads:
            need(r.w)
        for w in writes:
            need(w.w)
            for rv in w.r:
                need(rv)
        g = self.guard[eng]
        wlist = []
        for s, v in waits.items():
            if v > g.get(s, 0):
                g[s] = v
                wlist.append((self.sems[s], v))
        self.cnt[sem] += inc
        ev = Ev(sem, self.cnt[sem], None if asyn else eng)
        semh = self.sems[sem]
        n = len(fns)

        def runner(e, _fns=tuple(fns), _w=tuple(wlist), _semh=semh, _inc=inc, _n=n):
            for s, v in _w:
                e.wait_ge(s, v)
            for i, f in enumerate(_fns):
                inst = f(e)
                if i == _n - 1:
                    inst.then_inc(_semh, _inc)

        self.q[eng].append(runner)
        for r in reads:
            r.r.append(ev)
        for w in writes:
            w.w = ev
            w.r = []
        return ev

    def op(self, eng, fn, **kw):
        return self.grp(eng, [fn], **kw)


def build_program(stage=STAGE, tap_kv_flag=TAP_KV):
    nc = bass.Bass("TRN2", target_bir_lowering=False)
    AP = bass.AP

    def ein(name, shp, dt=F32):
        return nc.dram_tensor(name, shp, dt, kind="ExternalInput")

    x_in = ein("x_fm", [1024, TOK])
    wcT = ein("wcT", [16384, 1024])
    w_h = ein("w_h", [1024, 4096])
    w_o = ein("w_o", [2048, 1024])
    w_qk = ein("w_qk", [1024, 128])
    bd = ein("bd", [128, 4096])
    zl = ein("zl", [128, 8192])
    b_h_row = ein("b_h_row", [1, 4096])
    b_h_col = ein("b_h_col", [128, 32])
    b_o_row = ein("b_o_row", [1, 1024])
    b_qk_col = ein("b_qk_col", [128, 1])
    gam_t = ein("gam_t", [128, 4])
    bet_t = ein("bet_t", [128, 4])
    ng_col = ein("ng_col", [128, 8])
    nb_col = ein("nb_col", [128, 8])
    biasr = ein("biasr", [256, 256])
    onesc_b = ein("onesc_b", [128, 1], BF16)
    onesr_b = ein("onesr_b", [1, 128], BF16)
    onesr_f = ein("onesr_f", [1, 128])
    ones512_b = ein("ones512_b", [1, 512], BF16)
    ones512_f = ein("ones512_f", [1, 512])
    ident_b = ein("ident_b", [128, 128], BF16)
    ident_f = ein("ident_f", [128, 128])

    y_out = nc.dram_tensor("y_fm", [1024, TOK], F32, kind="ExternalOutput")
    tap_kv = (nc.dram_tensor("tap_kv", [128, 2048], F32, kind="ExternalOutput")
              if (tap_kv_flag or stage == 3) else None)

    wc_bf = nc.dram_tensor("wc_bf", [16384, 1024], BF16)
    wh_bf = nc.dram_tensor("wh_bf", [1024, 4096], BF16)
    wo_bf = nc.dram_tensor("wo_bf", [2048, 1024], BF16)
    xb_sp = nc.dram_tensor("xb_sp", [1024, TOK], F32)
    gate_sp = nc.dram_tensor("gate_sp", [2048, TOK], F32)
    qo_sp = nc.dram_tensor("qo_sp", [2048, TOK], F32)
    lq_sp = nc.dram_tensor("lq_sp", [128, TOK], BF16)
    cc_in = nc.dram_tensor("cc_in", [128, 2048], F32)
    cc_out = nc.dram_tensor("cc_out", [128, 2048], F32)

    es = ExitStack()
    with es:
        P = Prog(nc, es)

        def sb(name, shp, dt):
            return es.enter_context(nc.sbuf_tensor(name, shp, dt))

        def ps(name, shp, dt):
            return es.enter_context(nc.psum_tensor(name, shp, dt))

        sX = [sb(f"sX{c}", [128, TT], F32) for c in range(DC)]
        sAcc = [sb(f"sAcc{c}", [128, TT], F32) for c in range(DC)]
        sXn = [sb(f"sXn{c}", [128, TT], BF16) for c in range(DC)]
        sXb2 = [sb(f"sXb2{i}", [128, TT], BF16) for i in range(2)]
        sSq = [sb(f"sSq{i}", [128, TT], BF16) for i in range(2)]
        sY = [sb(f"sY{i}", [128, TT], BF16) for i in range(4)]
        NH = 16
        sH = [sb(f"sH{i}", [128, TT], BF16) for i in range(NH)]
        NWC = 12
        sWc = [sb(f"sWc{i}", [128, 512], BF16) for i in range(NWC)]
        sBD = sb("sBD", [128, 4096], BF16)
        sZl = sb("sZl", [128, 8192], BF16)
        sWqk = sb("sWqk", [128, 1024], BF16)
        sV = [sb(f"sV{i}", [128, 2048], BF16) for i in range(4)]
        sQs = sb("sQs", [128, TT], F32)
        sHd = [sb(f"sHd{i}", [128, TT], BF16) for i in range(3)]
        sHd3 = sb("sHd3", [128, TT], F32)
        sLkT = sb("sLkT", [128, TT], BF16)
        sAtn = [sb(f"sAtn{i}", [128, 256], BF16) for i in range(4)]
        sBias = [sb(f"sBias{i}", [128, 256], F32) for i in range(2)]
        sKv = sb("sKv", [128, 2048], F32)
        sKvB = sb("sKvB", [128, 2048], BF16)
        sGst = [sb(f"sGst{i}", [128, TT], F32) for i in range(4)]
        sQst = [sb(f"sQst{i}", [128, TT], F32) for i in range(4)]
        sQo = [sb(f"sQo{i}", [128, TT], F32) for i in range(4)]
        sT2 = [sb(f"sT2{i}", [128, TT], F32) for i in range(2)]
        sAB = [sb(f"sAB{i}", [128, TT], F32) for i in range(2)]
        sTb = [sb(f"sTb{i}", [128, TT], F32) for i in range(2)]
        sStat = sb("sStat", [1, 2560], F32)
        sRowB = sb("sRowB", [1, 4096], BF16)
        sRowB2 = sb("sRowB2", [1, 4096], BF16)
        sNgC = sb("sNgC", [128, 8], F32)
        sNbC = sb("sNbC", [128, 8], F32)
        sBhC = sb("sBhC", [128, 32], F32)
        sOneCB = sb("sOneCB", [128, 1], BF16)
        sOneRB = sb("sOneRB", [1, 128], BF16)
        sOneRF = sb("sOneRF", [1, 128], F32)
        sOne512B = sb("sOne512B", [1, 512], BF16)
        sIdentF = sb("sIdentF", [128, 128], F32)
        sBqk = sb("sBqk", [128, 1], F32)
        sGamT = sb("sGamT", [128, 4], F32)
        sBetT = sb("sBetT", [128, 4], F32)
        sGB0 = sb("sGB0", [128, 2], F32)

        pA = [ps(f"pA{i}", [128, 512], F32) for i in range(4)]
        pY = ps("pY", [128, 512], F32)
        pZ = ps("pZ", [128, 512], F32)
        pS = ps("pS", [128, 512], F32)
        pW = ps("pW", [128, 512], F32)

        rX = [Res() for _ in range(DC)]
        rAcc = [Res() for _ in range(DC)]
        rXn = [Res() for _ in range(DC)]
        rXb2 = [Res() for _ in range(2)]
        rSq = [Res() for _ in range(2)]
        rY = [Res() for _ in range(4)]
        rH = [Res() for _ in range(NH)]
        rWc = [Res() for _ in range(NWC)]
        rWhD = Res("d_pre")
        rWoD = Res("d_pre")
        rConst = Res()
        rV = [Res() for _ in range(4)]
        rQs = Res()
        rHd = [Res() for _ in range(4)]
        rLkT = Res()
        rAtn = [Res() for _ in range(4)]
        rKv = Res()
        rKvB = Res()
        rGst = [Res() for _ in range(4)]
        rQst = [Res() for _ in range(4)]
        rQo = [Res() for _ in range(4)]
        rT2 = [Res() for _ in range(2)]
        rAB = [Res() for _ in range(2)]
        rTb = [Res() for _ in range(2)]
        rStat = Res()
        rPA = [Res() for _ in range(4)]
        rPY = Res()
        rPZ = Res()
        rPS = Res()
        rPW = Res()
        rWcD = [Res() for _ in range(16)]
        rXbD = [Res("d_xb") for _ in range(NTILE)]
        rGateD = [Res("d_gate") for _ in range(NTILE)]
        rQoD = [Res("d_qo") for _ in range(NTILE)]
        rLqD = [Res("d_lq") for _ in range(NTILE)]
        rCc = Res("d_cc")

        def ld(dst_ap, src_ap, writes=(), q="act"):
            return P.op(q, lambda e, d=dst_ap, s=src_ap: e.dma_start(out=d, in_=s),
                        writes=writes, inc=16, asyn=True)

        ld(sOneCB[:, :], onesc_b.ap(), writes=[rConst])
        ld(sOneRB[:, :], onesr_b.ap(), writes=[rConst])
        ld(sOneRF[:, :], onesr_f.ap(), writes=[rConst])
        ld(sOne512B[:, :], ones512_b.ap(), writes=[rConst])
        ld(sIdentF[:, :], ident_f.ap(), writes=[rConst])
        ld(sBqk[:, :], b_qk_col.ap(), writes=[rConst])
        ld(sGamT[:, :], gam_t.ap(), writes=[rConst])
        ld(sBetT[:, :], bet_t.ap(), writes=[rConst])
        ld(sBhC[:, :], b_h_col.ap(), writes=[rConst])
        ld(sBias[0][:, :], biasr.ap()[0:128, :], writes=[rConst])
        ld(sBias[1][:, :], biasr.ap()[128:256, :], writes=[rConst])
        ld(sNgC[:, :], ng_col.ap(), writes=[rConst])
        ld(sNbC[:, :], nb_col.ap(), writes=[rConst])

        def cld(dst_ap, src_ap, writes=()):
            return P.op("pool", lambda e, d=dst_ap, s=src_ap: e.dma_start(out=d, in_=s),
                        writes=writes, inc=16, asyn=True)

        for ch in range(16):
            P.op("pool",
                 lambda e, ch=ch: e.dma_start(
                     out=AP(wc_bf, ch * 1024 * 1024, [[1024, 1024], [1, 1024]]),
                     in_=AP(wcT, ch * 1024 * 1024, [[1024, 1024], [1, 1024]])),
                 writes=[rWcD[ch]], inc=16, asyn=True)
        cld(sBD[:, :], bd.ap(), writes=[rConst])
        cld(sZl[:, :], zl.ap(), writes=[rConst])
        for k in range(DC):
            cld(sWqk[:, k * 128:(k + 1) * 128],
                AP(w_qk, k * 128 * 128, [[128, 128], [1, 128]]), writes=[rConst])
        cld(sRowB[0:1, :], b_h_row.ap(), writes=[rConst])
        cld(sRowB2[0:1, 0:1024], b_o_row.ap(), writes=[rConst])
        for ch in range(4):
            P.op("pool", lambda e, ch=ch: e.dma_start(
                    out=AP(wh_bf, ch * 256 * 4096, [[4096, 256], [1, 4096]]),
                    in_=AP(w_h, ch * 256 * 4096, [[4096, 256], [1, 4096]])),
                 writes=[rWhD], inc=16, asyn=True)
        for ch in range(4):
            P.op("pool", lambda e, ch=ch: e.dma_start(
                    out=AP(wo_bf, ch * 512 * 1024, [[1024, 512], [1, 1024]]),
                    in_=AP(w_o, ch * 512 * 1024, [[1024, 512], [1, 1024]])),
                 writes=[rWoD], inc=16, asyn=True)
        P.op("dve", lambda e: e.tensor_scalar(sGB0[:, 0:1], sGamT[:, 0:1],
                                              1.0 / 256.0, None, ALU.mult),
             reads=[rConst], writes=[rConst])
        P.op("dve", lambda e: e.tensor_scalar(sGB0[:, 1:2], sBetT[:, 0:1],
                                              1.0 / 256.0, None, ALU.mult),
             reads=[rConst], writes=[rConst])

        bh_row = sRowB[0:1, :]
        bo_row = sRowB2[0:1, 0:1024]
        ng_b = sRowB2[0:1, 1024:2048]
        nb_b = sRowB2[0:1, 2048:3072]

        st_mu = sStat[0:1, 0:512]
        st_ms = sStat[0:1, 512:1024]
        st_a = sStat[0:1, 1024:1536]
        st_b = sStat[0:1, 1536:2048]
        st_sc = sStat[0:1, 2048:2560]

        def layernorm(src, rSrc, fp32_bcast, affine, dst_f32=None, rDst_f32=None,
                      dst_bf=None, rDst_bf=None):
            for c in range(DC):
                i = c % 2
                P.op("act", lambda e, c=c, i=i: e.activation(
                        sXb2[i][:, :], src[c][:, :], AF.Copy),
                     reads=[rSrc[c]], writes=[rXb2[i]])
                P.op("dve", lambda e, c=c, i=i: e.tensor_tensor(
                        sSq[i][:, :], src[c][:, :], src[c][:, :], ALU.mult),
                     reads=[rSrc[c]], writes=[rSq[i]])
                P.grp("pe",
                      [lambda e, c=c, i=i: e.matmul(pS[0:1, :], sOneCB[:, :],
                                                    sXb2[i][:, :], start=(c == 0),
                                                    stop=(c == DC - 1)),
                       lambda e, c=c, i=i: e.matmul(pY[0:1, :], sOneCB[:, :],
                                                    sSq[i][:, :], start=(c == 0),
                                                    stop=(c == DC - 1))],
                      reads=[rXb2[i], rSq[i], rConst],
                      writes=([rPS, rPY] if c in (0, DC - 1) else []))
            warm(60)
            P.grp("dve",
                  [lambda e: e.tensor_scalar(st_mu, pS[0:1, :], 1.0 / 1024.0, None,
                                             ALU.mult),
                   lambda e: e.tensor_scalar(st_ms, pY[0:1, :], 1.0 / 1024.0, None,
                                             ALU.mult),
                   lambda e: e.tensor_tensor(st_sc, st_mu, st_mu, ALU.mult),
                   lambda e: e.scalar_tensor_tensor(st_sc, st_sc, -1.0, st_ms,
                                                    ALU.mult, ALU.add)],
                  reads=[rPS, rPY], writes=[rStat])
            P.op("act", lambda e: e.activation(st_sc, st_sc, AF.Sqrt,
                                               bias=np.float32(EPS)),
                 reads=[rStat], writes=[rStat])
            P.grp("dve",
                  [lambda e: e.reciprocal(st_a, st_sc),
                   lambda e: e.scalar_tensor_tensor(st_b, st_mu, -1.0, st_a,
                                                    ALU.mult, ALU.mult)],
                  reads=[rStat], writes=[rStat])
            P.op("pe", lambda e: e.matmul(pS[:, :], sOneRF[:, :], st_a,
                                          start=True, stop=True),
                 reads=[rStat, rConst], writes=[rPS])
            P.op("pe", lambda e: e.matmul(pY[:, :], sOneRF[:, :], st_b,
                                          start=True, stop=True),
                 reads=[rStat, rConst], writes=[rPY])
            P.op("act", lambda e: e.activation(sAB[0][:, :], pS[:, :], AF.Copy),
                 reads=[rPS], writes=[rAB[0]])
            P.op("act", lambda e: e.activation(sAB[1][:, :], pY[:, :], AF.Copy),
                 reads=[rPY], writes=[rAB[1]])
            dst = dst_f32 if dst_f32 is not None else dst_bf
            rDst = rDst_f32 if dst_f32 is not None else rDst_bf
            for c in range(DC):
                eng = "dve"
                i = c % 2
                fns = [lambda e, c=c, i=i: e.tensor_tensor(
                              sT2[i][:, :], src[c][:, :], sAB[0][:, :], ALU.mult)]
                if affine:
                    fns.append(lambda e, c=c, i=i: e.tensor_tensor(
                            sT2[i][:, :], sT2[i][:, :], sAB[1][:, :], ALU.add))
                    fns.append(lambda e, c=c, i=i: e.tensor_scalar(
                            dst[c][:, :], sT2[i][:, :],
                            sNgC[:, c:c + 1], sNbC[:, c:c + 1],
                            ALU.mult, ALU.add))
                else:
                    fns.append(lambda e, c=c, i=i, d=dst: e.tensor_tensor(
                            d[c][:, :], sT2[i][:, :], sAB[1][:, :], ALU.add))
                P.grp(eng, fns,
                      reads=[rSrc[c], rAB[0], rAB[1], rConst],
                      writes=[rT2[i], rDst[c]])

        def ffn():
            def unitsA(kb):
                ys = []
                zs = []
                head = kb // 2
                for yi in range(4):
                    oc0 = (kb % 2) * 16 + yi * 4
                    c_in = oc0 // 4
                    yslot = head * 8 + c_in
                    yb, ryb = (pY, rPY) if yi % 2 == 0 else (pS, rPS)

                    def em_y(yslot=yslot, c_in=c_in, yq=yi, yb=yb, ryb=ryb):
                        P.op("pe", lambda e: e.matmul(
                                yb[:, :], sBD[:, yslot * 128:(yslot + 1) * 128],
                                sXn[c_in][:, :], start=True, stop=True),
                             reads=[rXn[c_in], rConst], writes=[ryb])
                        P.op("act", lambda e: e.activation(
                                sY[yq][:, :], yb[:, :], AF.Copy),
                             reads=[ryb], writes=[rY[yq]])
                    ys.append(em_y)
                    for s in range(4):
                        oc = oc0 + s
                        w = (oc % 4) // 2
                        pb = 64 * w
                        cs = head * 16 + (oc // 4) * 2 + (oc % 2)
                        hslot = oc - (kb % 2) * 16
                        zb, rzb = (pZ, rPZ) if (4 * yi + s) % 2 == 0 else (pW, rPW)

                        def em_z(pb=pb, cs=cs, w=w, yq=yi, hslot=hslot, zb=zb,
                                 rzb=rzb):
                            P.op("pe", lambda e: e.matmul(
                                    zb[:, :],
                                    sZl[pb:pb + 64, cs * 128:(cs + 1) * 128],
                                    sY[yq][64 * w:64 * w + 64, :],
                                    start=True, stop=True),
                                 reads=[rY[yq], rConst], writes=[rzb])
                            P.op("act", lambda e: e.activation(
                                    sH[hslot][:, :], zb[:, :], AF.Gelu),
                                 reads=[rzb], writes=[rH[hslot]])
                        zs.append(em_z)
                us = [ys[0], ys[1]]
                for yi in range(4):
                    us.extend(zs[yi * 4:(yi + 1) * 4])
                    if yi + 2 < 4:
                        us.append(ys[yi + 2])
                return us

            for u in unitsA(0):
                u()
            for kb in range(8):
                nxt = unitsA(kb + 1) if kb < 7 else []
                ni = 0
                for fh in range(2):
                    for kk in range(16):
                        kg = kb * 16 + kk
                        slot = ring_dma(AP(wc_bf, kg * 128 * 1024 + fh * 512,
                                           [[1024, 128], [1, 512]]),
                                        rWcD[kg // 1024])
                        if kk == 0:
                            for fc in range(4):
                                P.op("pe", lambda e, slot=slot, fc=fc:
                                     e.matmul(pA[fc][:, :],
                                              sWc[slot][:, fc * 128:(fc + 1) * 128],
                                              sH[0][:, :],
                                              start=True, stop=False),
                                     reads=[rWc[slot], rH[0]], writes=[rPA[fc]])
                        else:
                            fns = []
                            for fc in range(4):
                                fns.append(lambda e, slot=slot, fc=fc, kk=kk:
                                           e.matmul(pA[fc][:, :],
                                                    sWc[slot][:, fc * 128:(fc + 1) * 128],
                                                    sH[kk][:, :],
                                                    start=False, stop=(kk == 15)))
                            P.grp("pe", fns,
                                  reads=[rWc[slot], rH[kk]],
                                  writes=([rPA[i] for i in range(4)]
                                          if kk == 15 else []))
                        if fh == 1:
                            want = (kk + 1) * len(nxt) // 16
                            while ni < want:
                                nxt[ni]()
                                ni += 1
                    for fc in range(4):
                        f = fh * 4 + fc
                        eng2 = "dve" if fc < 2 else "pool"
                        if kb == 0:
                            if fc < 2:
                                P.op("dve", lambda e, f=f, fc=fc: e.tensor_copy(
                                        sAcc[f][:, :], pA[fc][:, :]),
                                     reads=[rPA[fc]], writes=[rAcc[f]])
                            else:
                                P.op("act", lambda e, f=f, fc=fc: e.activation(
                                        sAcc[f][:, :], pA[fc][:, :], AF.Copy),
                                     reads=[rPA[fc]], writes=[rAcc[f]])
                        else:
                            if fc < 2:
                                P.op("dve", lambda e, f=f, fc=fc: e.tensor_tensor(
                                        sAcc[f][:, :], sAcc[f][:, :], pA[fc][:, :],
                                        ALU.add),
                                     reads=[rPA[fc]], writes=[rAcc[f]])
                            else:
                                P.op("dve", lambda e, f=f, fc=fc: e.tensor_tensor(
                                        sAcc[f][:, :], sAcc[f][:, :], pA[fc][:, :],
                                        ALU.add),
                                     reads=[rPA[fc]], writes=[rAcc[f]])

        def x_slice(dram, t0, c, width=TT):
            return AP(dram, (c * 128) * TOK + t0, [[TOK, 128], [1, width]])

        ring_idx = [0]
        def warm(n):
            for _ in range(n):
                P.op("pe", lambda e: e.matmul(pW[:, 0:128], sOneRB[:, :],
                                              sOneRB[:, :], start=True, stop=True),
                     reads=[rConst], writes=[rPW])

        def ring_dma(src_ap, rdep):
            slot = ring_idx[0] % NWC
            ring_idx[0] += 1
            P.op("sp", lambda e, slot=slot, s=src_ap: e.dma_start(
                    out=sWc[slot][:, :], in_=s),
                 reads=[rdep], writes=[rWc[slot]], inc=16, asyn=True)
            return slot

        def load_x(ti):
            t0 = ti * TT
            for c in range(DC):
                P.op("act", lambda e, c=c, t0=t0: e.dma_start(
                        out=sX[c][:, :], in_=x_slice(x_in, t0, c)),
                     writes=[rX[c]], inc=16, asyn=True)

        load_x(0)
        for ti in range(NTILE):
            t0 = ti * TT
            if stage <= 2 and ti > 0:
                load_x(ti)
            if ti == 0 or stage <= 2:
                layernorm(sX, rX, fp32_bcast=False, affine=False,
                          dst_bf=sXn, rDst_bf=rXn)
            ffn()
            for c in range(DC):
                P.op("dve", lambda e, c=c: e.scalar_tensor_tensor(
                        sX[c][:, :], sAcc[c][:, :], 0.5, sX[c][:, :],
                        ALU.mult, ALU.add),
                     reads=[rAcc[c]], writes=[rX[c]])
            if stage == 1:
                for c in range(DC):
                    P.op("act", lambda e, c=c, t0=t0: e.dma_start(
                            out=x_slice(y_out, t0, c), in_=sX[c][:, :]),
                         reads=[rX[c]], sem="d_out", inc=16, asyn=True)
                continue
            layernorm(sX, rX, fp32_bcast=True, affine=False,
                      dst_f32=sAcc, rDst_f32=rAcc)
            layernorm(sAcc, rAcc, fp32_bcast=False, affine=True,
                      dst_bf=sXn, rDst_bf=rXn)
            for c in range(DC):
                P.op("dve", lambda e, c=c: e.tensor_tensor(
                        sAcc[c][:, :], sAcc[c][:, :], sX[c][:, :], ALU.add),
                     reads=[rX[c]], writes=[rAcc[c]])
                P.op("pool", lambda e, c=c, t0=t0: e.dma_start(
                        out=x_slice(xb_sp, t0, c), in_=sAcc[c][:, :]),
                     reads=[rAcc[c]], writes=[rXbD[ti]], inc=16, asyn=True)
            if stage >= 3 and ti + 1 < NTILE:
                load_x(ti + 1)
            if stage == 2:
                for c in range(DC):
                    P.op("act", lambda e, c=c, t0=t0: e.dma_start(
                            out=x_slice(y_out, t0, c), in_=sAcc[c][:, :]),
                         reads=[rAcc[c]], sem="d_out", inc=16, asyn=True)
                continue
            quadA = ([pA[0], pA[1], pA[2], pA[3]], [rPA[0], rPA[1], rPA[2], rPA[3]])
            quadB = ([pY, pS, pZ, pW], [rPY, rPS, rPZ, rPW])
            for fc4 in range(4):
                bks, rbks = quadA if fc4 % 2 == 0 else quadB
                for K in range(DC):
                    slot = ring_dma(AP(wh_bf, K * 128 * 4096 + fc4 * 512,
                                       [[4096, 128], [1, 512]]), rWhD)
                    if K == 0:
                        for tb in range(4):
                            P.op("pe", lambda e, tb=tb, slot=slot, bks=bks:
                                 e.matmul(bks[tb][:, :],
                                          sXn[0][:, tb * 128:(tb + 1) * 128],
                                          sWc[slot][:, :],
                                          start=True, stop=False),
                                 reads=[rWc[slot], rXn[0]], writes=[rbks[tb]])
                        for tb in range(4):
                            P.op("pe", lambda e, fc4=fc4, tb=tb, bks=bks:
                                 e.matmul(bks[tb][:, :], sOneRB[:, :],
                                          bh_row[0:1, fc4 * 512:(fc4 + 1) * 512],
                                          start=False, stop=False),
                                 reads=[rConst], writes=[rbks[tb]])
                    else:
                        P.grp("pe",
                              [lambda e, tb=tb, K=K, slot=slot, bks=bks: e.matmul(
                                      bks[tb][:, :],
                                      sXn[K][:, tb * 128:(tb + 1) * 128],
                                      sWc[slot][:, :],
                                      start=False, stop=(K == DC - 1))
                               for tb in range(4)],
                              reads=[rWc[slot], rXn[K]],
                              writes=(rbks if K == DC - 1 else []))
                for tb in range(4):
                    P.op("act", lambda e, fc4=fc4, tb=tb, bks=bks: e.activation(
                            sV[tb][:, fc4 * 512:(fc4 + 1) * 512], bks[tb][:, :],
                            AF.Silu),
                         reads=[rbks[tb]], writes=[rV[tb]])
            for g4 in range(4):
                bks, rbks = quadA if g4 % 2 == 0 else quadB
                for K in range(DC):
                    slot = ring_dma(AP(wh_bf, K * 128 * 4096 + 2048 + g4 * 512,
                                       [[4096, 128], [1, 512]]), rWhD)
                    P.grp("pe",
                          [lambda e, g4=g4, K=K, slot=slot, fc=fc, bks=bks:
                           e.matmul(bks[fc][:, :],
                                    sWc[slot][:, fc * 128:(fc + 1) * 128],
                                    sXn[K][:, :],
                                    start=(K == 0), stop=(K == DC - 1))
                           for fc in range(4)],
                          reads=[rWc[slot], rXn[K]],
                          writes=(rbks if K in (0, DC - 1) else []))
                for fc in range(4):
                    fcg = g4 * 4 + fc
                    P.op("act", lambda e, fcg=fcg, fc=fc, bks=bks: e.activation(
                            sGst[fcg % 4][:, :], bks[fc][:, :], AF.Silu,
                            bias=sBhC[:, 16 + fcg:17 + fcg], scale=np.float32(1.0)),
                         reads=[rbks[fc], rConst], writes=[rGst[fcg % 4]])
                    P.op("pool", lambda e, fcg=fcg, t0=t0: e.dma_start(
                            out=AP(gate_sp, fcg * 128 * TOK + t0,
                                   [[TOK, 128], [1, TT]]),
                            in_=sGst[fcg % 4][:, :]),
                         reads=[rGst[fcg % 4]], writes=[rGateD[ti]], inc=16, asyn=True)
            fns = []
            for K in range(DC):
                fns.append(lambda e, K=K: e.matmul(
                        pZ[:, :], sWqk[:, K * 128:(K + 1) * 128], sXn[K][:, :],
                        start=(K == 0), stop=(K == DC - 1)))
            P.grp("pe", fns, reads=[rConst] + [rXn[K] for K in range(DC)],
                  writes=[rPZ])
            P.op("act", lambda e: e.activation(sQs[:, :], pZ[:, :], AF.Silu,
                                               bias=sBqk[:, :],
                                               scale=np.float32(1.0)),
                 reads=[rPZ, rConst], writes=[rQs])
            for h in range(4):
                ga = sGB0[:, 0:1] if h == 0 else sGamT[:, h:h + 1]
                be = sGB0[:, 1:2] if h == 0 else sBetT[:, h:h + 1]
                dst = sHd3 if h == 3 else sHd[h]
                P.op("dve", lambda e, h=h, ga=ga, be=be, dst=dst: e.tensor_scalar(
                        dst[:, :], sQs[:, :], ga, be, ALU.mult, ALU.add),
                     reads=[rQs, rConst], writes=[rHd[h]])
            P.op("pool", lambda e, t0=t0: e.dma_start(
                    out=AP(lq_sp, t0, [[TOK, 128], [1, TT]]), in_=sHd[1][:, :]),
                 reads=[rHd[1]], writes=[rLqD[ti]], inc=16, asyn=True)
            for tb in range(4):
                P.op("pe", lambda e, tb=tb: e.transpose(
                        pW[:, tb * 128:(tb + 1) * 128],
                        sHd3[:, tb * 128:(tb + 1) * 128], sIdentF[:, :]),
                     reads=[rHd[3], rConst], writes=[rPW])
            P.op("act", lambda e: e.activation(sLkT[:, :], pW[:, 0:512], AF.Copy),
                 reads=[rPW], writes=[rLkT])
            if ti + 1 < NTILE:
                layernorm(sX, rX, fp32_bcast=False, affine=False,
                          dst_bf=sXn, rDst_bf=rXn)
            for g in range(2):
                for jc in range(2):
                    pj = pZ if jc == 0 else pW
                    rpj = rPZ if jc == 0 else rPW
                    P.op("pe", lambda e, g=g, jc=jc, pj=pj: e.matmul(
                            pj[:, g * 256:(g + 1) * 256],
                            sHd[2][:, g * 256 + jc * 128:g * 256 + (jc + 1) * 128],
                            sHd[0][:, g * 256:(g + 1) * 256],
                            start=True, stop=True),
                         reads=[rHd[0], rHd[2]], writes=[rpj])
                    P.grp("dve",
                          [lambda e, g=g, jc=jc, pj=pj: e.scalar_tensor_tensor(
                                  sT2[jc][:, g * 256:(g + 1) * 256], sBias[jc][:, :],
                                  float(math.sqrt(128.0)),
                                  pj[:, g * 256:(g + 1) * 256],
                                  ALU.mult, ALU.add),
                           lambda e, g=g, jc=jc: e.scalar_tensor_tensor(
                                  sAtn[g * 2 + jc][:, :],
                                  sT2[jc][:, g * 256:(g + 1) * 256], 0.0,
                                  sT2[jc][:, g * 256:(g + 1) * 256],
                                  ALU.max, ALU.mult)],
                          reads=[rpj, rConst], writes=[rT2[jc], rAtn[g * 2 + jc]])
            for ec in range(16):
                bank = ec % 4
                fns = []
                for g in range(2):
                    for jc in range(2):
                        fns.append(lambda e, g=g, jc=jc, ec=ec, bank=bank: e.matmul(
                                pA[bank][:, g * 256:(g + 1) * 256],
                                sV[g * 2 + jc][:, ec * 128:(ec + 1) * 128],
                                sAtn[g * 2 + jc][:, :],
                                start=(jc == 0), stop=(jc == 1)))
                P.grp("pe", fns,
                      reads=[rV[0], rV[1], rV[2], rV[3]] + [rAtn[i] for i in range(4)],
                      writes=[rPA[bank]])
                P.op("act", lambda e, ec=ec, bank=bank: e.activation(
                        sQst[ec % 4][:, :], pA[bank][:, :], AF.Copy),
                     reads=[rPA[bank]], writes=[rQst[ec % 4]])
                P.op("pool", lambda e, ec=ec, t0=t0: e.dma_start(
                        out=AP(qo_sp, ec * 128 * TOK + t0,
                               [[TOK, 128], [1, TT]]),
                        in_=sQst[ec % 4][:, :]),
                     reads=[rQst[ec % 4]], writes=[rQoD[ti]],
                     inc=16, asyn=True)
            for ne in range(4):
                P.grp("pe",
                      [lambda e, ne=ne, tb=tb: e.matmul(
                              pA[ne][:, :],
                              sLkT[:, tb * 128:(tb + 1) * 128],
                              sV[tb][:, ne * 512:(ne + 1) * 512],
                              start=(tb == 0), stop=(tb == 3))
                       for tb in range(4)],
                      reads=[rLkT] + [rV[tb] for tb in range(4)],
                      writes=[rPA[ne]])
                if ti == 0:
                    P.op("dve", lambda e, ne=ne: e.tensor_copy(
                            sKv[:, ne * 512:(ne + 1) * 512], pA[ne][:, :]),
                         reads=[rPA[ne]], writes=[rKv])
                else:
                    P.op("dve", lambda e, ne=ne: e.tensor_tensor(
                            sKv[:, ne * 512:(ne + 1) * 512],
                            sKv[:, ne * 512:(ne + 1) * 512], pA[ne][:, :], ALU.add),
                         reads=[rPA[ne]], writes=[rKv])

        if stage >= 3:
            P.op("sp", lambda e: e.dma_start(out=cc_in.ap(), in_=sKv[:, :]),
                 reads=[rKv], writes=[rCc], inc=16, asyn=True)
            P.op("pool", lambda e: e.collective_compute(
                    "AllReduce", ALU.add,
                    replica_groups=[[0, 1], [2, 3], [4, 5], [6, 7]],
                    ins=[cc_in.ap()], outs=[cc_out.ap()]),
                 reads=[rCc], writes=[rCc], inc=1, asyn=True)
            P.op("sp", lambda e: e.dma_start(out=sKv[:, :], in_=cc_out.ap()),
                 reads=[rCc], writes=[rKv], inc=16, asyn=True)
            warm(800)
            P.op("act", lambda e: e.activation(sKvB[:, :], sKv[:, :], AF.Copy,
                                               scale=np.float32(1.0 / SEQ)),
                 reads=[rKv], writes=[rKvB])
            if tap_kv is not None:
                P.op("act", lambda e: e.dma_start(out=tap_kv.ap(), in_=sKv[:, :]),
                     reads=[rKv], sem="d_out", inc=16, asyn=True)

        if stage >= 4:
            def reload_xb_lq(ti):
                t0 = ti * TT
                for c in range(DC):
                    P.op("pool", lambda e, c=c, t0=t0: e.dma_start(
                            out=sX[c][:, :], in_=x_slice(xb_sp, t0, c)),
                         reads=[rXbD[ti]], writes=[rX[c]], inc=16, asyn=True)
                P.op("pool", lambda e, t0=t0: e.dma_start(
                        out=sHd[0][:, :],
                        in_=AP(lq_sp, t0, [[TOK, 128], [1, TT]])),
                     reads=[rLqD[ti]], writes=[rHd[0]], inc=16, asyn=True)

            def reload_qg(ti, ec):
                t0 = ti * TT
                P.op("pool", lambda e, ec=ec, t0=t0: e.dma_start(
                        out=sQo[ec % 4][:, :],
                        in_=AP(qo_sp, ec * 128 * TOK + t0,
                               [[TOK, 128], [1, TT]])),
                     reads=[rQoD[ti]], writes=[rQo[ec % 4]], inc=16, asyn=True)
                P.op("pool", lambda e, ec=ec, t0=t0: e.dma_start(
                        out=sGst[ec % 4][:, :],
                        in_=AP(gate_sp, ec * 128 * TOK + t0,
                               [[TOK, 128], [1, TT]])),
                     reads=[rGateD[ti]], writes=[rGst[ec % 4]],
                     inc=16, asyn=True)

            for ec in range(4):
                reload_qg(0, ec)
            for ti in range(NTILE):
                t0 = ti * TT
                reload_xb_lq(ti)
                for ec in range(16):
                    pj = pY if ec % 2 == 0 else pZ
                    rpj = rPY if ec % 2 == 0 else rPZ
                    P.op("pe", lambda e, ec=ec, pj=pj: e.matmul(
                            pj[:, :], sKvB[:, ec * 128:(ec + 1) * 128],
                            sHd[0][:, :], start=True, stop=True),
                         reads=[rKvB, rHd[0]], writes=[rpj])
                    P.grp("dve",
                          [lambda e, ec=ec, pj=pj: e.tensor_tensor(
                                  sTb[ec % 2][:, :], sQo[ec % 4][:, :], pj[:, :],
                                  ALU.add),
                           lambda e, ec=ec: e.tensor_tensor(
                                  sH[ec][:, :], sTb[ec % 2][:, :],
                                  sGst[ec % 4][:, :], ALU.mult)],
                          reads=[rpj, rGst[ec % 4], rQo[ec % 4]],
                          writes=[rTb[ec % 2], rH[ec]])
                    if ec + 4 < 16:
                        reload_qg(ti, ec + 4)
                    elif ti + 1 < NTILE:
                        reload_qg(ti + 1, ec - 12)
                banks = [pA[0], pA[1], pA[2], pA[3], pY, pZ, pS, pW]
                rbanks = [rPA[0], rPA[1], rPA[2], rPA[3], rPY, rPZ, rPS, rPW]
                for ec in range(16):
                    for half in range(2):
                        slot = ring_dma(AP(wo_bf, ec * 128 * 1024 + half * 512,
                                           [[1024, 128], [1, 512]]), rWoD)
                        P.grp("pe",
                              [lambda e, ec=ec, half=half, slot=slot, fc=fc:
                               e.matmul(banks[half * 4 + fc][:, :],
                                        sWc[slot][:, fc * 128:(fc + 1) * 128],
                                        sH[ec][:, :],
                                        start=(ec == 0), stop=False)
                               for fc in range(4)],
                              reads=[rWc[slot], rH[ec]],
                              writes=([rbanks[half * 4 + i] for i in range(4)]
                                      if ec == 0 else []))
                P.grp("pe",
                      [lambda e, f=f: e.matmul(
                              banks[f][:, :],
                              sRowB2[0:1, f * 128:(f + 1) * 128],
                              sOne512B[:, :], start=False, stop=True)
                       for f in range(8)],
                      reads=[rConst], writes=rbanks)
                for f in range(8):
                    P.op("dve", lambda e, f=f: e.tensor_tensor(
                            sX[f][:, :], sX[f][:, :], banks[f][:, :], ALU.add),
                         reads=[rbanks[f]], writes=[rX[f]])
                layernorm(sX, rX, fp32_bcast=False, affine=False,
                          dst_bf=sXn, rDst_bf=rXn)
                ffn()
                for c in range(DC):
                    P.op("dve", lambda e, c=c: e.scalar_tensor_tensor(
                            sX[c][:, :], sAcc[c][:, :], 0.5, sX[c][:, :],
                            ALU.mult, ALU.add),
                         reads=[rAcc[c]], writes=[rX[c]])
                    P.op("act", lambda e, c=c, t0=t0: e.dma_start(
                            out=x_slice(y_out, t0, c), in_=sX[c][:, :]),
                         reads=[rX[c]], sem="d_out", inc=16, asyn=True)

        out_total = P.cnt["d_out"]
        if out_total:
            P.q["sp"].append(lambda e, v=out_total, s=P.sems["d_out"]:
                             e.wait_ge(s, v))

        with nc.Block() as block:
            @block.tensor
            def _(t):
                for f in P.q["pe"]:
                    f(t)

            @block.scalar
            def _(s):
                for f in P.q["act"]:
                    f(s)

            @block.vector
            def _(v):
                for f in P.q["dve"]:
                    f(v)

            @block.gpsimd
            def _(g):
                for f in P.q["pool"]:
                    f(g)

            @block.sync
            def _(sy):
                for f in P.q["sp"]:
                    f(sy)

    return nc


def _bucket_table():
    GSZ, NB, MAXD = 256, 32, 128
    pos = np.arange(GSZ)
    n = pos[:, None] - pos[None, :]
    nbh = NB // 2
    ret = (n < 0).astype(np.int32) * nbh
    na = np.abs(n)
    max_exact = nbh // 2
    is_small = na < max_exact
    val_large = max_exact + (
        np.log(np.maximum(na, 1).astype(np.float32) / max_exact)
        / math.log(MAXD / max_exact) * (nbh - max_exact)).astype(np.int32)
    val_large = np.minimum(val_large, nbh - 1)
    return ret + np.where(is_small, na, val_large)


def _host_prep(inputs):
    x = np.asarray(inputs["L0_attn_x"], dtype=np.float32)
    rail_maker = np.asarray(inputs["L0_ffn_rail_maker"], np.float32)
    rail_up = np.asarray(inputs["L0_ffn_rail_up_proj"], np.float32)
    w_cproj = np.asarray(inputs["L0_ffn_w_cproj"], np.float32)
    W_h = np.asarray(inputs["L0_attn_W_h"], np.float32)
    W_qk = np.asarray(inputs["L0_attn_W_qk"], np.float32)
    W_o = np.asarray(inputs["L0_attn_W_o"], np.float32)
    b_h = np.asarray(inputs["L0_attn_b_h"], np.float32)
    b_qk = np.asarray(inputs["L0_attn_b_qk"], np.float32)
    b_o = np.asarray(inputs["L0_attn_b_o"], np.float32)
    qk_gamma = np.asarray(inputs["L0_attn_qk_gamma"], np.float32)
    qk_beta = np.asarray(inputs["L0_attn_qk_beta"], np.float32)
    norm_g = np.asarray(inputs["L0_attn_norm_g"], np.float32)
    norm_b = np.asarray(inputs["L0_attn_norm_b"], np.float32)
    rel_emb = np.asarray(inputs["L0_attn_rel_emb"], np.float32)

    xf = x.reshape(32768, 1024)
    shards = [np.ascontiguousarray(xf[c * TOK:(c + 1) * TOK].T)
              for c in range(NCORES)]
    wcT = np.ascontiguousarray(w_cproj.T)

    BD = np.zeros((128, 4096), np.float32)
    for h in range(4):
        for c in range(8):
            slot = h * 8 + c
            for rl in range(8):
                BD[rl * 16:(rl + 1) * 16,
                   slot * 128 + rl * 16: slot * 128 + (rl + 1) * 16] = \
                    rail_maker[h, 8 * c + rl]
    ZL = np.zeros((128, 8192), np.float32)
    for h in range(4):
        for oc in range(32):
            w = (oc % 4) // 2
            pb = 64 * w
            cs = h * 16 + (oc // 4) * 2 + (oc % 2)
            rr0 = 2 * (oc % 4) - 4 * w
            for rl2 in range(2):
                rail = 2 * oc + rl2
                rr = rr0 + rl2
                ZL[pb + rr * 16: pb + (rr + 1) * 16,
                   cs * 128 + rl2 * 64: cs * 128 + (rl2 + 1) * 64] = \
                    rail_up[h, rail].T

    bucketT = _bucket_table().T
    biasr = np.ascontiguousarray(rel_emb[bucketT, 0].astype(np.float32))

    common = dict(
        wcT=wcT, w_h=W_h, w_o=W_o, w_qk=W_qk, bd=BD, zl=ZL,
        b_h_row=np.ascontiguousarray(b_h.reshape(1, 4096)),
        b_h_col=np.ascontiguousarray(b_h.reshape(32, 128).T),
        b_o_row=np.ascontiguousarray(b_o.reshape(1, 1024)),
        b_qk_col=np.ascontiguousarray(b_qk.reshape(128, 1)),
        gam_t=np.ascontiguousarray(qk_gamma.T),
        bet_t=np.ascontiguousarray(qk_beta.T),
        ng_col=np.ascontiguousarray(norm_g.reshape(8, 128).T),
        nb_col=np.ascontiguousarray(norm_b.reshape(8, 128).T),
        biasr=biasr,
        onesc_b=np.ones((128, 1), bf16), onesr_b=np.ones((1, 128), bf16),
        onesr_f=np.ones((1, 128), np.float32),
        ones512_b=np.ones((1, 512), bf16),
        ones512_f=np.ones((1, 512), np.float32),
        ident_b=np.eye(128, dtype=bf16),
        ident_f=np.eye(128, dtype=np.float32),
    )
    return [dict(common, x_fm=shards[c]) for c in range(NCORES)]


_NC_CACHE = {}


def kernel(**inputs):
    in_maps = _host_prep(inputs)
    key = (STAGE, TAP_KV)
    if key not in _NC_CACHE:
        _NC_CACHE[key] = build_program(STAGE, TAP_KV)
    nc = _NC_CACHE[key]
    res = run_bass_kernel_spmd(nc, in_maps, core_ids=list(range(NCORES)))
    outs = [r["y_fm"] for r in res.results]
    full = np.concatenate([o.T for o in outs], axis=0)
    return np.ascontiguousarray(full.reshape(4, 8192, 1024), dtype=np.float32)


# version 12
# speedup vs baseline: 1.0318x; 1.0318x over previous
import math
from contextlib import ExitStack

import numpy as np
import ml_dtypes

import concourse.bass as bass
import concourse.mybir as mybir
from concourse.bass_utils import run_bass_kernel_spmd

F32 = mybir.dt.float32
BF16 = mybir.dt.bfloat16
AF = mybir.ActivationFunctionType
ALU = mybir.AluOpType
bf16 = ml_dtypes.bfloat16

NCORES = 8
TOK = 4096
TT = 512
NTILE = TOK // TT
DC = 8
SEQ = 8192
EPS = 1e-5

STAGE = 4
TAP_KV = False


class Ev:
    __slots__ = ("sem", "val", "eng")

    def __init__(self, sem, val, eng):
        self.sem, self.val, self.eng = sem, val, eng


class Res:
    __slots__ = ("w", "r", "sem")

    def __init__(self, sem=None):
        self.w = None
        self.r = []
        self.sem = sem


class Prog:
    ENGS = ("pe", "act", "dve", "pool", "sp")

    def __init__(self, nc, es):
        self.nc = nc
        self.q = {e: [] for e in self.ENGS}
        self.sems = {}
        self.cnt = {}
        self.guard = {e: {} for e in self.ENGS}
        self.es = es
        self._nres = 0
        for e in self.ENGS:
            self.new_sem("s_" + e)
        for d in ("d_pre", "d_xb", "d_gate", "d_qo", "d_lq", "d_cc", "d_out"):
            self.new_sem(d)

    def res_sem(self, res):
        if res.sem is None:
            res.sem = f"rs{self._nres}"
            self._nres += 1
            self.new_sem(res.sem)
        return res.sem

    def new_sem(self, name):
        self.sems[name] = self.es.enter_context(self.nc.semaphore(name))
        self.cnt[name] = 0

    def grp(self, eng, fns, reads=(), writes=(), sem=None, inc=1, asyn=False):
        if sem is None:
            sem = self.res_sem(writes[0]) if asyn else ("s_" + eng)
        waits = {}

        def need(ev):
            if ev is None:
                return
            if ev.eng is not None and ev.eng == eng:
                return
            v = self.cnt[ev.sem] if ev.eng is None else ev.val
            if v > waits.get(ev.sem, 0):
                waits[ev.sem] = v

        for r in reads:
            need(r.w)
        for w in writes:
            need(w.w)
            for rv in w.r:
                need(rv)
        g = self.guard[eng]
        wlist = []
        for s, v in waits.items():
            if v > g.get(s, 0):
                g[s] = v
                wlist.append((self.sems[s], v))
        self.cnt[sem] += inc
        ev = Ev(sem, self.cnt[sem], None if asyn else eng)
        semh = self.sems[sem]
        n = len(fns)

        def runner(e, _fns=tuple(fns), _w=tuple(wlist), _semh=semh, _inc=inc, _n=n):
            for s, v in _w:
                e.wait_ge(s, v)
            for i, f in enumerate(_fns):
                inst = f(e)
                if i == _n - 1:
                    inst.then_inc(_semh, _inc)

        self.q[eng].append(runner)
        for r in reads:
            r.r.append(ev)
        for w in writes:
            w.w = ev
            w.r = []
        return ev

    def op(self, eng, fn, **kw):
        return self.grp(eng, [fn], **kw)


def build_program(stage=STAGE, tap_kv_flag=TAP_KV):
    nc = bass.Bass("TRN2", target_bir_lowering=False)
    AP = bass.AP

    def ein(name, shp, dt=F32):
        return nc.dram_tensor(name, shp, dt, kind="ExternalInput")

    x_in = ein("x_fm", [1024, TOK])
    wcT = ein("wcT", [16384, 1024])
    w_h = ein("w_h", [1024, 4096])
    w_o = ein("w_o", [2048, 1024])
    w_qk = ein("w_qk", [1024, 128])
    bd = ein("bd", [128, 4096])
    zl = ein("zl", [128, 8192])
    b_h_row = ein("b_h_row", [1, 4096])
    b_h_col = ein("b_h_col", [128, 32])
    b_o_row = ein("b_o_row", [1, 1024])
    b_qk_col = ein("b_qk_col", [128, 1])
    gam_t = ein("gam_t", [128, 4])
    bet_t = ein("bet_t", [128, 4])
    ng_col = ein("ng_col", [128, 8])
    nb_col = ein("nb_col", [128, 8])
    biasr = ein("biasr", [256, 256])
    onesc_b = ein("onesc_b", [128, 1], BF16)
    onesr_b = ein("onesr_b", [1, 128], BF16)
    onesr_f = ein("onesr_f", [1, 128])
    ones512_b = ein("ones512_b", [1, 512], BF16)
    ones512_f = ein("ones512_f", [1, 512])
    ident_b = ein("ident_b", [128, 128], BF16)
    ident_f = ein("ident_f", [128, 128])

    y_out = nc.dram_tensor("y_fm", [1024, TOK], F32, kind="ExternalOutput")
    tap_kv = (nc.dram_tensor("tap_kv", [128, 2048], F32, kind="ExternalOutput")
              if (tap_kv_flag or stage == 3) else None)

    wc_bf = nc.dram_tensor("wc_bf", [16384, 1024], BF16)
    wh_bf = nc.dram_tensor("wh_bf", [1024, 4096], BF16)
    wo_bf = nc.dram_tensor("wo_bf", [2048, 1024], BF16)
    xb_sp = nc.dram_tensor("xb_sp", [1024, TOK], F32)
    gate_sp = nc.dram_tensor("gate_sp", [2048, TOK], F32)
    qo_sp = nc.dram_tensor("qo_sp", [2048, TOK], F32)
    lq_sp = nc.dram_tensor("lq_sp", [128, TOK], BF16)
    cc_in = nc.dram_tensor("cc_in", [128, 2048], F32)
    cc_out = nc.dram_tensor("cc_out", [128, 2048], F32)

    es = ExitStack()
    with es:
        P = Prog(nc, es)

        def sb(name, shp, dt):
            return es.enter_context(nc.sbuf_tensor(name, shp, dt))

        def ps(name, shp, dt):
            return es.enter_context(nc.psum_tensor(name, shp, dt))

        sX = [sb(f"sX{c}", [128, TT], F32) for c in range(DC)]
        sAcc = [sb(f"sAcc{c}", [128, TT], F32) for c in range(DC)]
        sXn = [sb(f"sXn{c}", [128, TT], BF16) for c in range(DC)]
        sXb2 = [sb(f"sXb2{i}", [128, TT], BF16) for i in range(2)]
        sSq = [sb(f"sSq{i}", [128, TT], BF16) for i in range(2)]
        sY = [sb(f"sY{i}", [128, TT], BF16) for i in range(4)]
        NH = 16
        sH = [sb(f"sH{i}", [128, TT], BF16) for i in range(NH)]
        NWC = 12
        sWc = [sb(f"sWc{i}", [128, 512], BF16) for i in range(NWC)]
        sBD = sb("sBD", [128, 4096], BF16)
        sZl = sb("sZl", [128, 8192], BF16)
        sWqk = sb("sWqk", [128, 1024], BF16)
        sV = [sb(f"sV{i}", [128, 2048], BF16) for i in range(4)]
        sQs = sb("sQs", [128, TT], F32)
        sHd = [sb(f"sHd{i}", [128, TT], BF16) for i in range(3)]
        sHd3 = sb("sHd3", [128, TT], F32)
        sLkT = sb("sLkT", [128, TT], BF16)
        sAtn = [sb(f"sAtn{i}", [128, 256], BF16) for i in range(4)]
        sBias = [sb(f"sBias{i}", [128, 256], F32) for i in range(2)]
        sKv = sb("sKv", [128, 2048], F32)
        sKvB = sb("sKvB", [128, 2048], BF16)
        sGst = [sb(f"sGst{i}", [128, TT], F32) for i in range(4)]
        sQst = [sb(f"sQst{i}", [128, TT], F32) for i in range(4)]
        sQo = [sb(f"sQo{i}", [128, TT], F32) for i in range(4)]
        sT2 = [sb(f"sT2{i}", [128, TT], F32) for i in range(2)]
        sAB = [sb(f"sAB{i}", [128, TT], F32) for i in range(2)]
        sTb = [sb(f"sTb{i}", [128, TT], F32) for i in range(2)]
        sStat = sb("sStat", [1, 2560], F32)
        sRowB = sb("sRowB", [1, 4096], BF16)
        sRowB2 = sb("sRowB2", [1, 4096], BF16)
        sNgC = sb("sNgC", [128, 8], F32)
        sNbC = sb("sNbC", [128, 8], F32)
        sBhC = sb("sBhC", [128, 32], F32)
        sOneCB = sb("sOneCB", [128, 1], BF16)
        sOneRB = sb("sOneRB", [1, 128], BF16)
        sOneRF = sb("sOneRF", [1, 128], F32)
        sOne512B = sb("sOne512B", [1, 512], BF16)
        sIdentF = sb("sIdentF", [128, 128], F32)
        sBqk = sb("sBqk", [128, 1], F32)
        sGamT = sb("sGamT", [128, 4], F32)
        sBetT = sb("sBetT", [128, 4], F32)
        sGB0 = sb("sGB0", [128, 2], F32)

        pA = [ps(f"pA{i}", [128, 512], F32) for i in range(4)]
        pY = ps("pY", [128, 512], F32)
        pZ = ps("pZ", [128, 512], F32)
        pS = ps("pS", [128, 512], F32)
        pW = ps("pW", [128, 512], F32)

        rX = [Res() for _ in range(DC)]
        rAcc = [Res() for _ in range(DC)]
        rXn = [Res() for _ in range(DC)]
        rXb2 = [Res() for _ in range(2)]
        rSq = [Res() for _ in range(2)]
        rY = [Res() for _ in range(4)]
        rH = [Res() for _ in range(NH)]
        rWc = [Res() for _ in range(NWC)]
        rWhD = Res("d_pre")
        rWoD = Res("d_pre")
        rConst = Res()
        rV = [Res() for _ in range(4)]
        rQs = Res()
        rHd = [Res() for _ in range(4)]
        rLkT = Res()
        rAtn = [Res() for _ in range(4)]
        rKv = Res()
        rKvB = Res()
        rGst = [Res() for _ in range(4)]
        rQst = [Res() for _ in range(4)]
        rQo = [Res() for _ in range(4)]
        rT2 = [Res() for _ in range(2)]
        rAB = [Res() for _ in range(2)]
        rTb = [Res() for _ in range(2)]
        rStat = Res()
        rPA = [Res() for _ in range(4)]
        rPY = Res()
        rPZ = Res()
        rPS = Res()
        rPW = Res()
        rWcD = [Res() for _ in range(16)]
        rXbD = [Res("d_xb") for _ in range(NTILE)]
        rGateD = [Res("d_gate") for _ in range(NTILE)]
        rQoD = [Res("d_qo") for _ in range(NTILE)]
        rLqD = [Res("d_lq") for _ in range(NTILE)]
        rCc = Res("d_cc")

        def ld(dst_ap, src_ap, writes=(), q="act"):
            return P.op(q, lambda e, d=dst_ap, s=src_ap: e.dma_start(out=d, in_=s),
                        writes=writes, inc=16, asyn=True)

        ld(sOneCB[:, :], onesc_b.ap(), writes=[rConst])
        ld(sOneRB[:, :], onesr_b.ap(), writes=[rConst])
        ld(sOneRF[:, :], onesr_f.ap(), writes=[rConst])
        ld(sOne512B[:, :], ones512_b.ap(), writes=[rConst])
        ld(sIdentF[:, :], ident_f.ap(), writes=[rConst])
        ld(sBqk[:, :], b_qk_col.ap(), writes=[rConst])
        ld(sGamT[:, :], gam_t.ap(), writes=[rConst])
        ld(sBetT[:, :], bet_t.ap(), writes=[rConst])
        ld(sBhC[:, :], b_h_col.ap(), writes=[rConst])
        ld(sBias[0][:, :], biasr.ap()[0:128, :], writes=[rConst])
        ld(sBias[1][:, :], biasr.ap()[128:256, :], writes=[rConst])
        ld(sNgC[:, :], ng_col.ap(), writes=[rConst])
        ld(sNbC[:, :], nb_col.ap(), writes=[rConst])

        def cld(dst_ap, src_ap, writes=()):
            return P.op("pool", lambda e, d=dst_ap, s=src_ap: e.dma_start(out=d, in_=s),
                        writes=writes, inc=16, asyn=True)

        for ch in range(16):
            P.op("pool",
                 lambda e, ch=ch: e.dma_start(
                     out=AP(wc_bf, ch * 1024 * 1024, [[1024, 1024], [1, 1024]]),
                     in_=AP(wcT, ch * 1024 * 1024, [[1024, 1024], [1, 1024]])),
                 writes=[rWcD[ch]], inc=16, asyn=True)
        cld(sBD[:, :], bd.ap(), writes=[rConst])
        cld(sZl[:, :], zl.ap(), writes=[rConst])
        for k in range(DC):
            cld(sWqk[:, k * 128:(k + 1) * 128],
                AP(w_qk, k * 128 * 128, [[128, 128], [1, 128]]), writes=[rConst])
        cld(sRowB[0:1, :], b_h_row.ap(), writes=[rConst])
        cld(sRowB2[0:1, 0:1024], b_o_row.ap(), writes=[rConst])
        for ch in range(4):
            P.op("pool", lambda e, ch=ch: e.dma_start(
                    out=AP(wh_bf, ch * 256 * 4096, [[4096, 256], [1, 4096]]),
                    in_=AP(w_h, ch * 256 * 4096, [[4096, 256], [1, 4096]])),
                 writes=[rWhD], inc=16, asyn=True)
        for ch in range(4):
            P.op("pool", lambda e, ch=ch: e.dma_start(
                    out=AP(wo_bf, ch * 512 * 1024, [[1024, 512], [1, 1024]]),
                    in_=AP(w_o, ch * 512 * 1024, [[1024, 512], [1, 1024]])),
                 writes=[rWoD], inc=16, asyn=True)
        P.op("dve", lambda e: e.tensor_scalar(sGB0[:, 0:1], sGamT[:, 0:1],
                                              1.0 / 256.0, None, ALU.mult),
             reads=[rConst], writes=[rConst])
        P.op("dve", lambda e: e.tensor_scalar(sGB0[:, 1:2], sBetT[:, 0:1],
                                              1.0 / 256.0, None, ALU.mult),
             reads=[rConst], writes=[rConst])

        bh_row = sRowB[0:1, :]
        bo_row = sRowB2[0:1, 0:1024]
        ng_b = sRowB2[0:1, 1024:2048]
        nb_b = sRowB2[0:1, 2048:3072]

        st_mu = sStat[0:1, 0:512]
        st_ms = sStat[0:1, 512:1024]
        st_a = sStat[0:1, 1024:1536]
        st_b = sStat[0:1, 1536:2048]
        st_sc = sStat[0:1, 2048:2560]

        def layernorm(src, rSrc, fp32_bcast, affine, dst_f32=None, rDst_f32=None,
                      dst_bf=None, rDst_bf=None):
            for c in range(DC):
                i = c % 2
                P.op("act", lambda e, c=c, i=i: e.activation(
                        sXb2[i][:, :], src[c][:, :], AF.Copy),
                     reads=[rSrc[c]], writes=[rXb2[i]])
                P.op("dve", lambda e, c=c, i=i: e.tensor_tensor(
                        sSq[i][:, :], src[c][:, :], src[c][:, :], ALU.mult),
                     reads=[rSrc[c]], writes=[rSq[i]])
                P.grp("pe",
                      [lambda e, c=c, i=i: e.matmul(pS[0:1, :], sOneCB[:, :],
                                                    sXb2[i][:, :], start=(c == 0),
                                                    stop=(c == DC - 1)),
                       lambda e, c=c, i=i: e.matmul(pY[0:1, :], sOneCB[:, :],
                                                    sSq[i][:, :], start=(c == 0),
                                                    stop=(c == DC - 1))],
                      reads=[rXb2[i], rSq[i], rConst],
                      writes=([rPS, rPY] if c in (0, DC - 1) else []))
            warm(60)
            P.grp("dve",
                  [lambda e: e.tensor_scalar(st_mu, pS[0:1, :], 1.0 / 1024.0, None,
                                             ALU.mult),
                   lambda e: e.tensor_scalar(st_ms, pY[0:1, :], 1.0 / 1024.0, None,
                                             ALU.mult),
                   lambda e: e.tensor_tensor(st_sc, st_mu, st_mu, ALU.mult),
                   lambda e: e.scalar_tensor_tensor(st_sc, st_sc, -1.0, st_ms,
                                                    ALU.mult, ALU.add)],
                  reads=[rPS, rPY], writes=[rStat])
            P.op("act", lambda e: e.activation(st_sc, st_sc, AF.Sqrt,
                                               bias=np.float32(EPS)),
                 reads=[rStat], writes=[rStat])
            P.grp("dve",
                  [lambda e: e.reciprocal(st_a, st_sc),
                   lambda e: e.scalar_tensor_tensor(st_b, st_mu, -1.0, st_a,
                                                    ALU.mult, ALU.mult)],
                  reads=[rStat], writes=[rStat])
            P.op("pe", lambda e: e.matmul(pS[:, :], sOneRF[:, :], st_a,
                                          start=True, stop=True),
                 reads=[rStat, rConst], writes=[rPS])
            P.op("pe", lambda e: e.matmul(pY[:, :], sOneRF[:, :], st_b,
                                          start=True, stop=True),
                 reads=[rStat, rConst], writes=[rPY])
            P.op("act", lambda e: e.activation(sAB[0][:, :], pS[:, :], AF.Copy),
                 reads=[rPS], writes=[rAB[0]])
            P.op("act", lambda e: e.activation(sAB[1][:, :], pY[:, :], AF.Copy),
                 reads=[rPY], writes=[rAB[1]])
            dst = dst_f32 if dst_f32 is not None else dst_bf
            rDst = rDst_f32 if dst_f32 is not None else rDst_bf
            for c in range(DC):
                eng = "dve"
                i = c % 2
                fns = [lambda e, c=c, i=i: e.tensor_tensor(
                              sT2[i][:, :], src[c][:, :], sAB[0][:, :], ALU.mult)]
                if affine:
                    fns.append(lambda e, c=c, i=i: e.tensor_tensor(
                            sT2[i][:, :], sT2[i][:, :], sAB[1][:, :], ALU.add))
                    fns.append(lambda e, c=c, i=i: e.tensor_scalar(
                            dst[c][:, :], sT2[i][:, :],
                            sNgC[:, c:c + 1], sNbC[:, c:c + 1],
                            ALU.mult, ALU.add))
                else:
                    fns.append(lambda e, c=c, i=i, d=dst: e.tensor_tensor(
                            d[c][:, :], sT2[i][:, :], sAB[1][:, :], ALU.add))
                P.grp(eng, fns,
                      reads=[rSrc[c], rAB[0], rAB[1], rConst],
                      writes=[rT2[i], rDst[c]])

        def ffn(post_evac=None, pre_slots=None):
            def unitsA(kb):
                ys = []
                zs = []
                head = kb // 2
                for yi in range(4):
                    oc0 = (kb % 2) * 16 + yi * 4
                    c_in = oc0 // 4
                    yslot = head * 8 + c_in
                    yb, ryb = (pY, rPY) if yi % 2 == 0 else (pS, rPS)

                    def em_y(yslot=yslot, c_in=c_in, yq=yi, yb=yb, ryb=ryb):
                        P.op("pe", lambda e: e.matmul(
                                yb[:, :], sBD[:, yslot * 128:(yslot + 1) * 128],
                                sXn[c_in][:, :], start=True, stop=True),
                             reads=[rXn[c_in], rConst], writes=[ryb])
                        P.op("act", lambda e: e.activation(
                                sY[yq][:, :], yb[:, :], AF.Copy),
                             reads=[ryb], writes=[rY[yq]])
                    ys.append(em_y)
                    for s in range(4):
                        oc = oc0 + s
                        w = (oc % 4) // 2
                        pb = 64 * w
                        cs = head * 16 + (oc // 4) * 2 + (oc % 2)
                        hslot = oc - (kb % 2) * 16
                        zb, rzb = (pZ, rPZ) if (4 * yi + s) % 2 == 0 else (pW, rPW)

                        def em_z(pb=pb, cs=cs, w=w, yq=yi, hslot=hslot, zb=zb,
                                 rzb=rzb):
                            P.op("pe", lambda e: e.matmul(
                                    zb[:, :],
                                    sZl[pb:pb + 64, cs * 128:(cs + 1) * 128],
                                    sY[yq][64 * w:64 * w + 64, :],
                                    start=True, stop=True),
                                 reads=[rY[yq], rConst], writes=[rzb])
                            P.op("act", lambda e: e.activation(
                                    sH[hslot][:, :], zb[:, :], AF.Gelu),
                                 reads=[rzb], writes=[rH[hslot]])
                        zs.append(em_z)
                us = [ys[0], ys[1]]
                for yi in range(4):
                    us.extend(zs[yi * 4:(yi + 1) * 4])
                    if yi + 2 < 4:
                        us.append(ys[yi + 2])
                return us

            for u in unitsA(0):
                u()
            for kb in range(8):
                nxt = unitsA(kb + 1) if kb < 7 else []
                ni = 0
                for fh in range(2):
                    for kk in range(16):
                        kg = kb * 16 + kk
                        pidx = fh * 16 + kk
                        if pre_slots is not None and kb == 0 and pidx < len(pre_slots):
                            slot = pre_slots[pidx]
                        else:
                            slot = ring_dma(AP(wc_bf, kg * 128 * 1024 + fh * 512,
                                               [[1024, 128], [1, 512]]),
                                            rWcD[kg // 1024])
                        if kk == 0:
                            for fc in range(4):
                                P.op("pe", lambda e, slot=slot, fc=fc:
                                     e.matmul(pA[fc][:, :],
                                              sWc[slot][:, fc * 128:(fc + 1) * 128],
                                              sH[0][:, :],
                                              start=True, stop=False),
                                     reads=[rWc[slot], rH[0]], writes=[rPA[fc]])
                        else:
                            fns = []
                            for fc in range(4):
                                fns.append(lambda e, slot=slot, fc=fc, kk=kk:
                                           e.matmul(pA[fc][:, :],
                                                    sWc[slot][:, fc * 128:(fc + 1) * 128],
                                                    sH[kk][:, :],
                                                    start=False, stop=(kk == 15)))
                            P.grp("pe", fns,
                                  reads=[rWc[slot], rH[kk]],
                                  writes=([rPA[i] for i in range(4)]
                                          if kk == 15 else []))
                        if fh == 1:
                            want = (kk + 1) * len(nxt) // 16
                            while ni < want:
                                nxt[ni]()
                                ni += 1
                    for fc in range(4):
                        f = fh * 4 + fc
                        if kb == 0:
                            if fc < 2:
                                P.op("dve", lambda e, f=f, fc=fc: e.tensor_copy(
                                        sAcc[f][:, :], pA[fc][:, :]),
                                     reads=[rPA[fc]], writes=[rAcc[f]])
                            else:
                                P.op("act", lambda e, f=f, fc=fc: e.activation(
                                        sAcc[f][:, :], pA[fc][:, :], AF.Copy),
                                     reads=[rPA[fc]], writes=[rAcc[f]])
                        else:
                            P.op("dve", lambda e, f=f, fc=fc: e.tensor_tensor(
                                    sAcc[f][:, :], sAcc[f][:, :], pA[fc][:, :],
                                    ALU.add),
                                 reads=[rPA[fc]], writes=[rAcc[f]])
                        if kb == 7 and post_evac is not None:
                            post_evac(f)

        def x_slice(dram, t0, c, width=TT):
            return AP(dram, (c * 128) * TOK + t0, [[TOK, 128], [1, width]])

        ring_idx = [0]
        def warm(n):
            for _ in range(n):
                P.op("pe", lambda e: e.matmul(pW[:, 0:128], sOneRB[:, :],
                                              sOneRB[:, :], start=True, stop=True),
                     reads=[rConst], writes=[rPW])

        def ring_dma(src_ap, rdep):
            slot = ring_idx[0] % NWC
            ring_idx[0] += 1
            P.op("sp", lambda e, slot=slot, s=src_ap: e.dma_start(
                    out=sWc[slot][:, :], in_=s),
                 reads=[rdep], writes=[rWc[slot]], inc=16, asyn=True)
            return slot

        def load_x(ti):
            t0 = ti * TT
            for c in range(DC):
                P.op("act", lambda e, c=c, t0=t0: e.dma_start(
                        out=sX[c][:, :], in_=x_slice(x_in, t0, c)),
                     writes=[rX[c]], inc=16, asyn=True)

        load_x(0)
        for ti in range(NTILE):
            t0 = ti * TT
            if stage <= 2 and ti > 0:
                load_x(ti)
            if ti == 0 or stage <= 2:
                layernorm(sX, rX, fp32_bcast=False, affine=False,
                          dst_bf=sXn, rDst_bf=rXn)
            def x1_add(c):
                P.op("dve", lambda e, c=c: e.scalar_tensor_tensor(
                        sX[c][:, :], sAcc[c][:, :], 0.5, sX[c][:, :],
                        ALU.mult, ALU.add),
                     reads=[rAcc[c]], writes=[rX[c]])
            ffn(post_evac=x1_add, pre_slots=pre1 if ti > 0 else None)
            if stage == 1:
                for c in range(DC):
                    P.op("act", lambda e, c=c, t0=t0: e.dma_start(
                            out=x_slice(y_out, t0, c), in_=sX[c][:, :]),
                         reads=[rX[c]], sem="d_out", inc=16, asyn=True)
                continue
            layernorm(sX, rX, fp32_bcast=True, affine=False,
                      dst_f32=sAcc, rDst_f32=rAcc)
            layernorm(sAcc, rAcc, fp32_bcast=False, affine=True,
                      dst_bf=sXn, rDst_bf=rXn)
            for c in range(DC):
                P.op("dve", lambda e, c=c: e.tensor_tensor(
                        sAcc[c][:, :], sAcc[c][:, :], sX[c][:, :], ALU.add),
                     reads=[rX[c]], writes=[rAcc[c]])
                P.op("pool", lambda e, c=c, t0=t0: e.dma_start(
                        out=x_slice(xb_sp, t0, c), in_=sAcc[c][:, :]),
                     reads=[rAcc[c]], writes=[rXbD[ti]], inc=16, asyn=True)
            if stage >= 3 and ti + 1 < NTILE:
                load_x(ti + 1)
            if stage == 2:
                for c in range(DC):
                    P.op("act", lambda e, c=c, t0=t0: e.dma_start(
                            out=x_slice(y_out, t0, c), in_=sAcc[c][:, :]),
                         reads=[rAcc[c]], sem="d_out", inc=16, asyn=True)
                continue
            quadA = ([pA[0], pA[1], pA[2], pA[3]], [rPA[0], rPA[1], rPA[2], rPA[3]])
            quadB = ([pY, pS, pZ, pW], [rPY, rPS, rPZ, rPW])
            for fc4 in range(4):
                bks, rbks = quadA if fc4 % 2 == 0 else quadB
                for K in range(DC):
                    slot = ring_dma(AP(wh_bf, K * 128 * 4096 + fc4 * 512,
                                       [[4096, 128], [1, 512]]), rWhD)
                    if K == 0:
                        for tb in range(4):
                            P.op("pe", lambda e, tb=tb, slot=slot, bks=bks:
                                 e.matmul(bks[tb][:, :],
                                          sXn[0][:, tb * 128:(tb + 1) * 128],
                                          sWc[slot][:, :],
                                          start=True, stop=False),
                                 reads=[rWc[slot], rXn[0]], writes=[rbks[tb]])
                        for tb in range(4):
                            P.op("pe", lambda e, fc4=fc4, tb=tb, bks=bks:
                                 e.matmul(bks[tb][:, :], sOneRB[:, :],
                                          bh_row[0:1, fc4 * 512:(fc4 + 1) * 512],
                                          start=False, stop=False),
                                 reads=[rConst], writes=[rbks[tb]])
                    else:
                        P.grp("pe",
                              [lambda e, tb=tb, K=K, slot=slot, bks=bks: e.matmul(
                                      bks[tb][:, :],
                                      sXn[K][:, tb * 128:(tb + 1) * 128],
                                      sWc[slot][:, :],
                                      start=False, stop=(K == DC - 1))
                               for tb in range(4)],
                              reads=[rWc[slot], rXn[K]],
                              writes=(rbks if K == DC - 1 else []))
                for tb in range(4):
                    P.op("act", lambda e, fc4=fc4, tb=tb, bks=bks: e.activation(
                            sV[tb][:, fc4 * 512:(fc4 + 1) * 512], bks[tb][:, :],
                            AF.Silu),
                         reads=[rbks[tb]], writes=[rV[tb]])
            for g4 in range(4):
                bks, rbks = quadA if g4 % 2 == 0 else quadB
                for K in range(DC):
                    slot = ring_dma(AP(wh_bf, K * 128 * 4096 + 2048 + g4 * 512,
                                       [[4096, 128], [1, 512]]), rWhD)
                    P.grp("pe",
                          [lambda e, g4=g4, K=K, slot=slot, fc=fc, bks=bks:
                           e.matmul(bks[fc][:, :],
                                    sWc[slot][:, fc * 128:(fc + 1) * 128],
                                    sXn[K][:, :],
                                    start=(K == 0), stop=(K == DC - 1))
                           for fc in range(4)],
                          reads=[rWc[slot], rXn[K]],
                          writes=(rbks if K in (0, DC - 1) else []))
                for fc in range(4):
                    fcg = g4 * 4 + fc
                    P.op("act", lambda e, fcg=fcg, fc=fc, bks=bks: e.activation(
                            sGst[fcg % 4][:, :], bks[fc][:, :], AF.Silu,
                            bias=sBhC[:, 16 + fcg:17 + fcg], scale=np.float32(1.0)),
                         reads=[rbks[fc], rConst], writes=[rGst[fcg % 4]])
                    P.op("pool", lambda e, fcg=fcg, t0=t0: e.dma_start(
                            out=AP(gate_sp, fcg * 128 * TOK + t0,
                                   [[TOK, 128], [1, TT]]),
                            in_=sGst[fcg % 4][:, :]),
                         reads=[rGst[fcg % 4]], writes=[rGateD[ti]], inc=16, asyn=True)
            fns = []
            for K in range(DC):
                fns.append(lambda e, K=K: e.matmul(
                        pZ[:, :], sWqk[:, K * 128:(K + 1) * 128], sXn[K][:, :],
                        start=(K == 0), stop=(K == DC - 1)))
            P.grp("pe", fns, reads=[rConst] + [rXn[K] for K in range(DC)],
                  writes=[rPZ])
            P.op("act", lambda e: e.activation(sQs[:, :], pZ[:, :], AF.Silu,
                                               bias=sBqk[:, :],
                                               scale=np.float32(1.0)),
                 reads=[rPZ, rConst], writes=[rQs])
            for h in range(4):
                ga = sGB0[:, 0:1] if h == 0 else sGamT[:, h:h + 1]
                be = sGB0[:, 1:2] if h == 0 else sBetT[:, h:h + 1]
                dst = sHd3 if h == 3 else sHd[h]
                P.op("dve", lambda e, h=h, ga=ga, be=be, dst=dst: e.tensor_scalar(
                        dst[:, :], sQs[:, :], ga, be, ALU.mult, ALU.add),
                     reads=[rQs, rConst], writes=[rHd[h]])
            P.op("pool", lambda e, t0=t0: e.dma_start(
                    out=AP(lq_sp, t0, [[TOK, 128], [1, TT]]), in_=sHd[1][:, :]),
                 reads=[rHd[1]], writes=[rLqD[ti]], inc=16, asyn=True)
            for tb in range(4):
                P.op("pe", lambda e, tb=tb: e.transpose(
                        pW[:, tb * 128:(tb + 1) * 128],
                        sHd3[:, tb * 128:(tb + 1) * 128], sIdentF[:, :]),
                     reads=[rHd[3], rConst], writes=[rPW])
            P.op("act", lambda e: e.activation(sLkT[:, :], pW[:, 0:512], AF.Copy),
                 reads=[rPW], writes=[rLkT])
            if ti + 1 < NTILE:
                layernorm(sX, rX, fp32_bcast=False, affine=False,
                          dst_bf=sXn, rDst_bf=rXn)
                pre1 = [ring_dma(AP(wc_bf, kk * 128 * 1024, [[1024, 128], [1, 512]]),
                                 rWcD[0]) for kk in range(8)]
            else:
                pre1 = None
            for g in range(2):
                for jc in range(2):
                    pj = pZ if jc == 0 else pW
                    rpj = rPZ if jc == 0 else rPW
                    P.op("pe", lambda e, g=g, jc=jc, pj=pj: e.matmul(
                            pj[:, g * 256:(g + 1) * 256],
                            sHd[2][:, g * 256 + jc * 128:g * 256 + (jc + 1) * 128],
                            sHd[0][:, g * 256:(g + 1) * 256],
                            start=True, stop=True),
                         reads=[rHd[0], rHd[2]], writes=[rpj])
                    P.grp("dve",
                          [lambda e, g=g, jc=jc, pj=pj: e.scalar_tensor_tensor(
                                  sT2[jc][:, g * 256:(g + 1) * 256], sBias[jc][:, :],
                                  float(math.sqrt(128.0)),
                                  pj[:, g * 256:(g + 1) * 256],
                                  ALU.mult, ALU.add),
                           lambda e, g=g, jc=jc: e.scalar_tensor_tensor(
                                  sAtn[g * 2 + jc][:, :],
                                  sT2[jc][:, g * 256:(g + 1) * 256], 0.0,
                                  sT2[jc][:, g * 256:(g + 1) * 256],
                                  ALU.max, ALU.mult)],
                          reads=[rpj, rConst], writes=[rT2[jc], rAtn[g * 2 + jc]])
            for ec in range(16):
                bank = ec % 4
                fns = []
                for g in range(2):
                    for jc in range(2):
                        fns.append(lambda e, g=g, jc=jc, ec=ec, bank=bank: e.matmul(
                                pA[bank][:, g * 256:(g + 1) * 256],
                                sV[g * 2 + jc][:, ec * 128:(ec + 1) * 128],
                                sAtn[g * 2 + jc][:, :],
                                start=(jc == 0), stop=(jc == 1)))
                P.grp("pe", fns,
                      reads=[rV[0], rV[1], rV[2], rV[3]] + [rAtn[i] for i in range(4)],
                      writes=[rPA[bank]])
                P.op("act", lambda e, ec=ec, bank=bank: e.activation(
                        sQst[ec % 4][:, :], pA[bank][:, :], AF.Copy),
                     reads=[rPA[bank]], writes=[rQst[ec % 4]])
                P.op("pool", lambda e, ec=ec, t0=t0: e.dma_start(
                        out=AP(qo_sp, ec * 128 * TOK + t0,
                               [[TOK, 128], [1, TT]]),
                        in_=sQst[ec % 4][:, :]),
                     reads=[rQst[ec % 4]], writes=[rQoD[ti]],
                     inc=16, asyn=True)
            for ne in range(4):
                P.grp("pe",
                      [lambda e, ne=ne, tb=tb: e.matmul(
                              pA[ne][:, :],
                              sLkT[:, tb * 128:(tb + 1) * 128],
                              sV[tb][:, ne * 512:(ne + 1) * 512],
                              start=(tb == 0), stop=(tb == 3))
                       for tb in range(4)],
                      reads=[rLkT] + [rV[tb] for tb in range(4)],
                      writes=[rPA[ne]])
                if ti == 0:
                    P.op("dve", lambda e, ne=ne: e.tensor_copy(
                            sKv[:, ne * 512:(ne + 1) * 512], pA[ne][:, :]),
                         reads=[rPA[ne]], writes=[rKv])
                else:
                    P.op("dve", lambda e, ne=ne: e.tensor_tensor(
                            sKv[:, ne * 512:(ne + 1) * 512],
                            sKv[:, ne * 512:(ne + 1) * 512], pA[ne][:, :], ALU.add),
                         reads=[rPA[ne]], writes=[rKv])

        if stage >= 3:
            P.op("sp", lambda e: e.dma_start(out=cc_in.ap(), in_=sKv[:, :]),
                 reads=[rKv], writes=[rCc], inc=16, asyn=True)
            P.op("pool", lambda e: e.collective_compute(
                    "AllReduce", ALU.add,
                    replica_groups=[[0, 1], [2, 3], [4, 5], [6, 7]],
                    ins=[cc_in.ap()], outs=[cc_out.ap()]),
                 reads=[rCc], writes=[rCc], inc=1, asyn=True)
            P.op("sp", lambda e: e.dma_start(out=sKv[:, :], in_=cc_out.ap()),
                 reads=[rCc], writes=[rKv], inc=16, asyn=True)
            warm(800)
            P.op("act", lambda e: e.activation(sKvB[:, :], sKv[:, :], AF.Copy,
                                               scale=np.float32(1.0 / SEQ)),
                 reads=[rKv], writes=[rKvB])
            if tap_kv is not None:
                P.op("act", lambda e: e.dma_start(out=tap_kv.ap(), in_=sKv[:, :]),
                     reads=[rKv], sem="d_out", inc=16, asyn=True)

        if stage >= 4:
            def reload_xb_lq(ti):
                t0 = ti * TT
                for c in range(DC):
                    P.op("pool", lambda e, c=c, t0=t0: e.dma_start(
                            out=sX[c][:, :], in_=x_slice(xb_sp, t0, c)),
                         reads=[rXbD[ti]], writes=[rX[c]], inc=16, asyn=True)
                P.op("pool", lambda e, t0=t0: e.dma_start(
                        out=sHd[0][:, :],
                        in_=AP(lq_sp, t0, [[TOK, 128], [1, TT]])),
                     reads=[rLqD[ti]], writes=[rHd[0]], inc=16, asyn=True)

            def reload_qg(ti, ec):
                t0 = ti * TT
                P.op("pool", lambda e, ec=ec, t0=t0: e.dma_start(
                        out=sQo[ec % 4][:, :],
                        in_=AP(qo_sp, ec * 128 * TOK + t0,
                               [[TOK, 128], [1, TT]])),
                     reads=[rQoD[ti]], writes=[rQo[ec % 4]], inc=16, asyn=True)
                P.op("pool", lambda e, ec=ec, t0=t0: e.dma_start(
                        out=sGst[ec % 4][:, :],
                        in_=AP(gate_sp, ec * 128 * TOK + t0,
                               [[TOK, 128], [1, TT]])),
                     reads=[rGateD[ti]], writes=[rGst[ec % 4]],
                     inc=16, asyn=True)

            for ec in range(4):
                reload_qg(0, ec)
            for ti in range(NTILE):
                t0 = ti * TT
                reload_xb_lq(ti)
                for ec in range(16):
                    pj = pY if ec % 2 == 0 else pZ
                    rpj = rPY if ec % 2 == 0 else rPZ
                    P.op("pe", lambda e, ec=ec, pj=pj: e.matmul(
                            pj[:, :], sKvB[:, ec * 128:(ec + 1) * 128],
                            sHd[0][:, :], start=True, stop=True),
                         reads=[rKvB, rHd[0]], writes=[rpj])
                    P.grp("dve",
                          [lambda e, ec=ec, pj=pj: e.tensor_tensor(
                                  sTb[ec % 2][:, :], sQo[ec % 4][:, :], pj[:, :],
                                  ALU.add),
                           lambda e, ec=ec: e.tensor_tensor(
                                  sH[ec][:, :], sTb[ec % 2][:, :],
                                  sGst[ec % 4][:, :], ALU.mult)],
                          reads=[rpj, rGst[ec % 4], rQo[ec % 4]],
                          writes=[rTb[ec % 2], rH[ec]])
                    if ec + 4 < 16:
                        reload_qg(ti, ec + 4)
                    elif ti + 1 < NTILE:
                        reload_qg(ti + 1, ec - 12)
                banks = [pA[0], pA[1], pA[2], pA[3], pY, pZ, pS, pW]
                rbanks = [rPA[0], rPA[1], rPA[2], rPA[3], rPY, rPZ, rPS, rPW]
                for ec in range(16):
                    for half in range(2):
                        slot = ring_dma(AP(wo_bf, ec * 128 * 1024 + half * 512,
                                           [[1024, 128], [1, 512]]), rWoD)
                        P.grp("pe",
                              [lambda e, ec=ec, half=half, slot=slot, fc=fc:
                               e.matmul(banks[half * 4 + fc][:, :],
                                        sWc[slot][:, fc * 128:(fc + 1) * 128],
                                        sH[ec][:, :],
                                        start=(ec == 0), stop=False)
                               for fc in range(4)],
                              reads=[rWc[slot], rH[ec]],
                              writes=([rbanks[half * 4 + i] for i in range(4)]
                                      if ec == 0 else []))
                P.grp("pe",
                      [lambda e, f=f: e.matmul(
                              banks[f][:, :],
                              sRowB2[0:1, f * 128:(f + 1) * 128],
                              sOne512B[:, :], start=False, stop=True)
                       for f in range(8)],
                      reads=[rConst], writes=rbanks)
                for f in range(8):
                    P.op("dve", lambda e, f=f: e.tensor_tensor(
                            sX[f][:, :], sX[f][:, :], banks[f][:, :], ALU.add),
                         reads=[rbanks[f]], writes=[rX[f]])
                pre2 = [ring_dma(AP(wc_bf, kk * 128 * 1024,
                                     [[1024, 128], [1, 512]]), rWcD[0])
                        for kk in range(8)]
                layernorm(sX, rX, fp32_bcast=False, affine=False,
                          dst_bf=sXn, rDst_bf=rXn)
                def fin_add(c, t0=t0):
                    P.op("dve", lambda e, c=c: e.scalar_tensor_tensor(
                            sX[c][:, :], sAcc[c][:, :], 0.5, sX[c][:, :],
                            ALU.mult, ALU.add),
                         reads=[rAcc[c]], writes=[rX[c]])
                    P.op("act", lambda e, c=c, t0=t0: e.dma_start(
                            out=x_slice(y_out, t0, c), in_=sX[c][:, :]),
                         reads=[rX[c]], sem="d_out", inc=16, asyn=True)
                ffn(post_evac=fin_add, pre_slots=pre2)

        out_total = P.cnt["d_out"]
        if out_total:
            P.q["sp"].append(lambda e, v=out_total, s=P.sems["d_out"]:
                             e.wait_ge(s, v))

        with nc.Block() as block:
            @block.tensor
            def _(t):
                for f in P.q["pe"]:
                    f(t)

            @block.scalar
            def _(s):
                for f in P.q["act"]:
                    f(s)

            @block.vector
            def _(v):
                for f in P.q["dve"]:
                    f(v)

            @block.gpsimd
            def _(g):
                for f in P.q["pool"]:
                    f(g)

            @block.sync
            def _(sy):
                for f in P.q["sp"]:
                    f(sy)

    return nc


def _bucket_table():
    GSZ, NB, MAXD = 256, 32, 128
    pos = np.arange(GSZ)
    n = pos[:, None] - pos[None, :]
    nbh = NB // 2
    ret = (n < 0).astype(np.int32) * nbh
    na = np.abs(n)
    max_exact = nbh // 2
    is_small = na < max_exact
    val_large = max_exact + (
        np.log(np.maximum(na, 1).astype(np.float32) / max_exact)
        / math.log(MAXD / max_exact) * (nbh - max_exact)).astype(np.int32)
    val_large = np.minimum(val_large, nbh - 1)
    return ret + np.where(is_small, na, val_large)


def _host_prep(inputs):
    x = np.asarray(inputs["L0_attn_x"], dtype=np.float32)
    rail_maker = np.asarray(inputs["L0_ffn_rail_maker"], np.float32)
    rail_up = np.asarray(inputs["L0_ffn_rail_up_proj"], np.float32)
    w_cproj = np.asarray(inputs["L0_ffn_w_cproj"], np.float32)
    W_h = np.asarray(inputs["L0_attn_W_h"], np.float32)
    W_qk = np.asarray(inputs["L0_attn_W_qk"], np.float32)
    W_o = np.asarray(inputs["L0_attn_W_o"], np.float32)
    b_h = np.asarray(inputs["L0_attn_b_h"], np.float32)
    b_qk = np.asarray(inputs["L0_attn_b_qk"], np.float32)
    b_o = np.asarray(inputs["L0_attn_b_o"], np.float32)
    qk_gamma = np.asarray(inputs["L0_attn_qk_gamma"], np.float32)
    qk_beta = np.asarray(inputs["L0_attn_qk_beta"], np.float32)
    norm_g = np.asarray(inputs["L0_attn_norm_g"], np.float32)
    norm_b = np.asarray(inputs["L0_attn_norm_b"], np.float32)
    rel_emb = np.asarray(inputs["L0_attn_rel_emb"], np.float32)

    xf = x.reshape(32768, 1024)
    shards = [np.ascontiguousarray(xf[c * TOK:(c + 1) * TOK].T)
              for c in range(NCORES)]
    wcT = np.ascontiguousarray(w_cproj.T)

    BD = np.zeros((128, 4096), np.float32)
    for h in range(4):
        for c in range(8):
            slot = h * 8 + c
            for rl in range(8):
                BD[rl * 16:(rl + 1) * 16,
                   slot * 128 + rl * 16: slot * 128 + (rl + 1) * 16] = \
                    rail_maker[h, 8 * c + rl]
    ZL = np.zeros((128, 8192), np.float32)
    for h in range(4):
        for oc in range(32):
            w = (oc % 4) // 2
            pb = 64 * w
            cs = h * 16 + (oc // 4) * 2 + (oc % 2)
            rr0 = 2 * (oc % 4) - 4 * w
            for rl2 in range(2):
                rail = 2 * oc + rl2
                rr = rr0 + rl2
                ZL[pb + rr * 16: pb + (rr + 1) * 16,
                   cs * 128 + rl2 * 64: cs * 128 + (rl2 + 1) * 64] = \
                    rail_up[h, rail].T

    bucketT = _bucket_table().T
    biasr = np.ascontiguousarray(rel_emb[bucketT, 0].astype(np.float32))

    common = dict(
        wcT=wcT, w_h=W_h, w_o=W_o, w_qk=W_qk, bd=BD, zl=ZL,
        b_h_row=np.ascontiguousarray(b_h.reshape(1, 4096)),
        b_h_col=np.ascontiguousarray(b_h.reshape(32, 128).T),
        b_o_row=np.ascontiguousarray(b_o.reshape(1, 1024)),
        b_qk_col=np.ascontiguousarray(b_qk.reshape(128, 1)),
        gam_t=np.ascontiguousarray(qk_gamma.T),
        bet_t=np.ascontiguousarray(qk_beta.T),
        ng_col=np.ascontiguousarray(norm_g.reshape(8, 128).T),
        nb_col=np.ascontiguousarray(norm_b.reshape(8, 128).T),
        biasr=biasr,
        onesc_b=np.ones((128, 1), bf16), onesr_b=np.ones((1, 128), bf16),
        onesr_f=np.ones((1, 128), np.float32),
        ones512_b=np.ones((1, 512), bf16),
        ones512_f=np.ones((1, 512), np.float32),
        ident_b=np.eye(128, dtype=bf16),
        ident_f=np.eye(128, dtype=np.float32),
    )
    return [dict(common, x_fm=shards[c]) for c in range(NCORES)]


_NC_CACHE = {}


def kernel(**inputs):
    in_maps = _host_prep(inputs)
    key = (STAGE, TAP_KV)
    if key not in _NC_CACHE:
        _NC_CACHE[key] = build_program(STAGE, TAP_KV)
    nc = _NC_CACHE[key]
    res = run_bass_kernel_spmd(nc, in_maps, core_ids=list(range(NCORES)))
    outs = [r["y_fm"] for r in res.results]
    full = np.concatenate([o.T for o in outs], axis=0)
    return np.ascontiguousarray(full.reshape(4, 8192, 1024), dtype=np.float32)


# version 13
# speedup vs baseline: 1.0424x; 1.0103x over previous
import math
from contextlib import ExitStack

import numpy as np
import ml_dtypes

import concourse.bass as bass
import concourse.mybir as mybir
from concourse.bass_utils import run_bass_kernel_spmd

F32 = mybir.dt.float32
BF16 = mybir.dt.bfloat16
AF = mybir.ActivationFunctionType
ALU = mybir.AluOpType
bf16 = ml_dtypes.bfloat16

NCORES = 8
TOK = 4096
TT = 512
NTILE = TOK // TT
DC = 8
SEQ = 8192
EPS = 1e-5

STAGE = 4
TAP_KV = False


class Ev:
    __slots__ = ("sem", "val", "eng")

    def __init__(self, sem, val, eng):
        self.sem, self.val, self.eng = sem, val, eng


class Res:
    __slots__ = ("w", "r", "sem")

    def __init__(self, sem=None):
        self.w = None
        self.r = []
        self.sem = sem


class Prog:
    ENGS = ("pe", "act", "dve", "pool", "sp")

    def __init__(self, nc, es):
        self.nc = nc
        self.q = {e: [] for e in self.ENGS}
        self.sems = {}
        self.cnt = {}
        self.guard = {e: {} for e in self.ENGS}
        self.es = es
        self._nres = 0
        for e in self.ENGS:
            self.new_sem("s_" + e)
        for d in ("d_pre", "d_xb", "d_gate", "d_qo", "d_lq", "d_cc", "d_out"):
            self.new_sem(d)

    def res_sem(self, res):
        if res.sem is None:
            res.sem = f"rs{self._nres}"
            self._nres += 1
            self.new_sem(res.sem)
        return res.sem

    def new_sem(self, name):
        self.sems[name] = self.es.enter_context(self.nc.semaphore(name))
        self.cnt[name] = 0

    def grp(self, eng, fns, reads=(), writes=(), sem=None, inc=1, asyn=False):
        if sem is None:
            sem = self.res_sem(writes[0]) if asyn else ("s_" + eng)
        waits = {}

        def need(ev):
            if ev is None:
                return
            if ev.eng is not None and ev.eng == eng:
                return
            v = self.cnt[ev.sem] if ev.eng is None else ev.val
            if v > waits.get(ev.sem, 0):
                waits[ev.sem] = v

        for r in reads:
            need(r.w)
        for w in writes:
            need(w.w)
            for rv in w.r:
                need(rv)
        g = self.guard[eng]
        wlist = []
        for s, v in waits.items():
            if v > g.get(s, 0):
                g[s] = v
                wlist.append((self.sems[s], v))
        self.cnt[sem] += inc
        ev = Ev(sem, self.cnt[sem], None if asyn else eng)
        semh = self.sems[sem]
        n = len(fns)

        def runner(e, _fns=tuple(fns), _w=tuple(wlist), _semh=semh, _inc=inc, _n=n):
            for s, v in _w:
                e.wait_ge(s, v)
            for i, f in enumerate(_fns):
                inst = f(e)
                if i == _n - 1:
                    inst.then_inc(_semh, _inc)

        self.q[eng].append(runner)
        for r in reads:
            r.r.append(ev)
        for w in writes:
            w.w = ev
            w.r = []
        return ev

    def op(self, eng, fn, **kw):
        return self.grp(eng, [fn], **kw)


def build_program(stage=STAGE, tap_kv_flag=TAP_KV):
    nc = bass.Bass("TRN2", target_bir_lowering=False)
    AP = bass.AP

    def ein(name, shp, dt=F32):
        return nc.dram_tensor(name, shp, dt, kind="ExternalInput")

    x_in = ein("x_fm", [1024, TOK])
    wcT = ein("wcT", [16384, 1024])
    w_h = ein("w_h", [1024, 4096])
    w_o = ein("w_o", [2048, 1024])
    w_qk = ein("w_qk", [1024, 128])
    bd = ein("bd", [128, 4096])
    zl = ein("zl", [128, 8192])
    b_h_row = ein("b_h_row", [1, 4096])
    b_h_col = ein("b_h_col", [128, 32])
    b_o_row = ein("b_o_row", [1, 1024])
    b_qk_col = ein("b_qk_col", [128, 1])
    gam_t = ein("gam_t", [128, 4])
    bet_t = ein("bet_t", [128, 4])
    ng_col = ein("ng_col", [128, 8])
    nb_col = ein("nb_col", [128, 8])
    biasr = ein("biasr", [256, 256])
    onesc_b = ein("onesc_b", [128, 1], BF16)
    onesr_b = ein("onesr_b", [1, 128], BF16)
    onesr_f = ein("onesr_f", [1, 128])
    ones512_b = ein("ones512_b", [1, 512], BF16)
    ones512_f = ein("ones512_f", [1, 512])
    ident_b = ein("ident_b", [128, 128], BF16)
    ident_f = ein("ident_f", [128, 128])

    y_out = nc.dram_tensor("y_fm", [1024, TOK], F32, kind="ExternalOutput")
    tap_kv = (nc.dram_tensor("tap_kv", [128, 2048], F32, kind="ExternalOutput")
              if (tap_kv_flag or stage == 3) else None)

    wc_bf = nc.dram_tensor("wc_bf", [16384, 1024], BF16)
    wh_bf = nc.dram_tensor("wh_bf", [1024, 4096], BF16)
    wo_bf = nc.dram_tensor("wo_bf", [2048, 1024], BF16)
    xb_sp = nc.dram_tensor("xb_sp", [1024, TOK], F32)
    gate_sp = nc.dram_tensor("gate_sp", [2048, TOK], F32)
    qo_sp = nc.dram_tensor("qo_sp", [2048, TOK], F32)
    lq_sp = nc.dram_tensor("lq_sp", [128, TOK], BF16)
    cc_in = nc.dram_tensor("cc_in", [128, 2048], F32)
    cc_out = nc.dram_tensor("cc_out", [128, 2048], F32)

    es = ExitStack()
    with es:
        P = Prog(nc, es)

        def sb(name, shp, dt):
            return es.enter_context(nc.sbuf_tensor(name, shp, dt))

        def ps(name, shp, dt):
            return es.enter_context(nc.psum_tensor(name, shp, dt))

        sX = [sb(f"sX{c}", [128, TT], F32) for c in range(DC)]
        sAcc = [sb(f"sAcc{c}", [128, TT], F32) for c in range(DC)]
        sXn = [sb(f"sXn{c}", [128, TT], BF16) for c in range(DC)]
        sXb2 = [sb(f"sXb2{i}", [128, TT], BF16) for i in range(2)]
        sSq = [sb(f"sSq{i}", [128, TT], BF16) for i in range(2)]
        sY = [sb(f"sY{i}", [128, TT], BF16) for i in range(4)]
        NH = 16
        sH = [sb(f"sH{i}", [128, TT], BF16) for i in range(NH)]
        NWC = 12
        sWc = [sb(f"sWc{i}", [128, 512], BF16) for i in range(NWC)]
        sBD = sb("sBD", [128, 4096], BF16)
        sZl = sb("sZl", [128, 8192], BF16)
        sWqk = sb("sWqk", [128, 1024], BF16)
        sV = [sb(f"sV{i}", [128, 2048], BF16) for i in range(4)]
        sQs = sb("sQs", [128, TT], F32)
        sHd = [sb(f"sHd{i}", [128, TT], BF16) for i in range(3)]
        sHd3 = sb("sHd3", [128, TT], F32)
        sLkT = sb("sLkT", [128, TT], BF16)
        sAtn = [sb(f"sAtn{i}", [128, 256], BF16) for i in range(4)]
        sBias = [sb(f"sBias{i}", [128, 256], F32) for i in range(2)]
        sKv = sb("sKv", [128, 2048], F32)
        sKvB = sb("sKvB", [128, 2048], BF16)
        sGst = [sb(f"sGst{i}", [128, TT], F32) for i in range(4)]
        sQst = [sb(f"sQst{i}", [128, TT], F32) for i in range(4)]
        sQo = [sb(f"sQo{i}", [128, TT], F32) for i in range(4)]
        sT2 = [sb(f"sT2{i}", [128, TT], F32) for i in range(2)]
        sAB = [sb(f"sAB{i}", [128, TT], F32) for i in range(2)]
        sTb = [sb(f"sTb{i}", [128, TT], F32) for i in range(2)]
        sStat = sb("sStat", [1, 2560], F32)
        sRowB = sb("sRowB", [1, 4096], BF16)
        sRowB2 = sb("sRowB2", [1, 4096], BF16)
        sNgC = sb("sNgC", [128, 8], F32)
        sNbC = sb("sNbC", [128, 8], F32)
        sBhC = sb("sBhC", [128, 32], F32)
        sOneCB = sb("sOneCB", [128, 1], BF16)
        sOneRB = sb("sOneRB", [1, 128], BF16)
        sOneRF = sb("sOneRF", [1, 128], F32)
        sOne512B = sb("sOne512B", [1, 512], BF16)
        sIdentF = sb("sIdentF", [128, 128], F32)
        sBqk = sb("sBqk", [128, 1], F32)
        sGamT = sb("sGamT", [128, 4], F32)
        sBetT = sb("sBetT", [128, 4], F32)
        sGB0 = sb("sGB0", [128, 2], F32)

        pA = [ps(f"pA{i}", [128, 512], F32) for i in range(4)]
        pY = ps("pY", [128, 512], F32)
        pZ = ps("pZ", [128, 512], F32)
        pS = ps("pS", [128, 512], F32)
        pW = ps("pW", [128, 512], F32)

        rX = [Res() for _ in range(DC)]
        rAcc = [Res() for _ in range(DC)]
        rXn = [Res() for _ in range(DC)]
        rXb2 = [Res() for _ in range(2)]
        rSq = [Res() for _ in range(2)]
        rY = [Res() for _ in range(4)]
        rH = [Res() for _ in range(NH)]
        rWc = [Res() for _ in range(NWC)]
        rWhD = Res("d_pre")
        rWoD = Res("d_pre")
        rConst = Res()
        rV = [Res() for _ in range(4)]
        rQs = Res()
        rHd = [Res() for _ in range(4)]
        rLkT = Res()
        rAtn = [Res() for _ in range(4)]
        rKv = Res()
        rKvB = Res()
        rGst = [Res() for _ in range(4)]
        rQst = [Res() for _ in range(4)]
        rQo = [Res() for _ in range(4)]
        rT2 = [Res() for _ in range(2)]
        rAB = [Res() for _ in range(2)]
        rTb = [Res() for _ in range(2)]
        rStat = Res()
        rPA = [Res() for _ in range(4)]
        rPY = Res()
        rPZ = Res()
        rPS = Res()
        rPW = Res()
        rWcD = [Res() for _ in range(16)]
        rXbD = [Res("d_xb") for _ in range(NTILE)]
        rGateD = [Res("d_gate") for _ in range(NTILE)]
        rQoD = [Res("d_qo") for _ in range(NTILE)]
        rLqD = [Res("d_lq") for _ in range(NTILE)]
        rCc = Res("d_cc")

        def ld(dst_ap, src_ap, writes=(), q="act"):
            return P.op(q, lambda e, d=dst_ap, s=src_ap: e.dma_start(out=d, in_=s),
                        writes=writes, inc=16, asyn=True)

        ld(sOneCB[:, :], onesc_b.ap(), writes=[rConst])
        ld(sOneRB[:, :], onesr_b.ap(), writes=[rConst])
        ld(sOneRF[:, :], onesr_f.ap(), writes=[rConst])
        ld(sOne512B[:, :], ones512_b.ap(), writes=[rConst])
        ld(sIdentF[:, :], ident_f.ap(), writes=[rConst])
        ld(sBqk[:, :], b_qk_col.ap(), writes=[rConst])
        ld(sGamT[:, :], gam_t.ap(), writes=[rConst])
        ld(sBetT[:, :], bet_t.ap(), writes=[rConst])
        ld(sBhC[:, :], b_h_col.ap(), writes=[rConst])
        ld(sBias[0][:, :], biasr.ap()[0:128, :], writes=[rConst])
        ld(sBias[1][:, :], biasr.ap()[128:256, :], writes=[rConst])
        ld(sNgC[:, :], ng_col.ap(), writes=[rConst])
        ld(sNbC[:, :], nb_col.ap(), writes=[rConst])

        def cld(dst_ap, src_ap, writes=()):
            return P.op("pool", lambda e, d=dst_ap, s=src_ap: e.dma_start(out=d, in_=s),
                        writes=writes, inc=16, asyn=True)

        for ch in range(16):
            P.op("pool",
                 lambda e, ch=ch: e.dma_start(
                     out=AP(wc_bf, ch * 1024 * 1024, [[1024, 1024], [1, 1024]]),
                     in_=AP(wcT, ch * 1024 * 1024, [[1024, 1024], [1, 1024]])),
                 writes=[rWcD[ch]], inc=16, asyn=True)
        cld(sBD[:, :], bd.ap(), writes=[rConst])
        cld(sZl[:, :], zl.ap(), writes=[rConst])
        for k in range(DC):
            cld(sWqk[:, k * 128:(k + 1) * 128],
                AP(w_qk, k * 128 * 128, [[128, 128], [1, 128]]), writes=[rConst])
        cld(sRowB[0:1, :], b_h_row.ap(), writes=[rConst])
        cld(sRowB2[0:1, 0:1024], b_o_row.ap(), writes=[rConst])
        for ch in range(4):
            P.op("pool", lambda e, ch=ch: e.dma_start(
                    out=AP(wh_bf, ch * 256 * 4096, [[4096, 256], [1, 4096]]),
                    in_=AP(w_h, ch * 256 * 4096, [[4096, 256], [1, 4096]])),
                 writes=[rWhD], inc=16, asyn=True)
        for ch in range(4):
            P.op("pool", lambda e, ch=ch: e.dma_start(
                    out=AP(wo_bf, ch * 512 * 1024, [[1024, 512], [1, 1024]]),
                    in_=AP(w_o, ch * 512 * 1024, [[1024, 512], [1, 1024]])),
                 writes=[rWoD], inc=16, asyn=True)
        P.op("dve", lambda e: e.tensor_scalar(sGB0[:, 0:1], sGamT[:, 0:1],
                                              1.0 / 256.0, None, ALU.mult),
             reads=[rConst], writes=[rConst])
        P.op("dve", lambda e: e.tensor_scalar(sGB0[:, 1:2], sBetT[:, 0:1],
                                              1.0 / 256.0, None, ALU.mult),
             reads=[rConst], writes=[rConst])

        bh_row = sRowB[0:1, :]
        bo_row = sRowB2[0:1, 0:1024]
        ng_b = sRowB2[0:1, 1024:2048]
        nb_b = sRowB2[0:1, 2048:3072]

        st_mu = sStat[0:1, 0:512]
        st_ms = sStat[0:1, 512:1024]
        st_a = sStat[0:1, 1024:1536]
        st_b = sStat[0:1, 1536:2048]
        st_sc = sStat[0:1, 2048:2560]

        def layernorm(src, rSrc, fp32_bcast, affine, dst_f32=None, rDst_f32=None,
                      dst_bf=None, rDst_bf=None):
            for c in range(DC):
                i = c % 2
                P.op("act", lambda e, c=c, i=i: e.activation(
                        sXb2[i][:, :], src[c][:, :], AF.Copy),
                     reads=[rSrc[c]], writes=[rXb2[i]])
                P.op("dve", lambda e, c=c, i=i: e.tensor_tensor(
                        sSq[i][:, :], src[c][:, :], src[c][:, :], ALU.mult),
                     reads=[rSrc[c]], writes=[rSq[i]])
                P.grp("pe",
                      [lambda e, c=c, i=i: e.matmul(pS[0:1, :], sOneCB[:, :],
                                                    sXb2[i][:, :], start=(c == 0),
                                                    stop=(c == DC - 1)),
                       lambda e, c=c, i=i: e.matmul(pY[0:1, :], sOneCB[:, :],
                                                    sSq[i][:, :], start=(c == 0),
                                                    stop=(c == DC - 1))],
                      reads=[rXb2[i], rSq[i], rConst],
                      writes=([rPS, rPY] if c in (0, DC - 1) else []))
            warm(60)
            P.grp("dve",
                  [lambda e: e.tensor_scalar(st_mu, pS[0:1, :], 1.0 / 1024.0, None,
                                             ALU.mult),
                   lambda e: e.tensor_scalar(st_ms, pY[0:1, :], 1.0 / 1024.0, None,
                                             ALU.mult),
                   lambda e: e.tensor_tensor(st_sc, st_mu, st_mu, ALU.mult),
                   lambda e: e.scalar_tensor_tensor(st_sc, st_sc, -1.0, st_ms,
                                                    ALU.mult, ALU.add)],
                  reads=[rPS, rPY], writes=[rStat])
            P.op("act", lambda e: e.activation(st_sc, st_sc, AF.Sqrt,
                                               bias=np.float32(EPS)),
                 reads=[rStat], writes=[rStat])
            P.grp("dve",
                  [lambda e: e.reciprocal(st_a, st_sc),
                   lambda e: e.scalar_tensor_tensor(st_b, st_mu, -1.0, st_a,
                                                    ALU.mult, ALU.mult)],
                  reads=[rStat], writes=[rStat])
            P.op("pe", lambda e: e.matmul(pS[:, :], sOneRF[:, :], st_a,
                                          start=True, stop=True),
                 reads=[rStat, rConst], writes=[rPS])
            P.op("pe", lambda e: e.matmul(pY[:, :], sOneRF[:, :], st_b,
                                          start=True, stop=True),
                 reads=[rStat, rConst], writes=[rPY])
            P.op("act", lambda e: e.activation(sAB[0][:, :], pS[:, :], AF.Copy),
                 reads=[rPS], writes=[rAB[0]])
            P.op("act", lambda e: e.activation(sAB[1][:, :], pY[:, :], AF.Copy),
                 reads=[rPY], writes=[rAB[1]])
            dst = dst_f32 if dst_f32 is not None else dst_bf
            rDst = rDst_f32 if dst_f32 is not None else rDst_bf
            for c in range(DC):
                eng = "dve"
                i = c % 2
                fns = [lambda e, c=c, i=i: e.tensor_tensor(
                              sT2[i][:, :], src[c][:, :], sAB[0][:, :], ALU.mult)]
                if affine:
                    fns.append(lambda e, c=c, i=i: e.tensor_tensor(
                            sT2[i][:, :], sT2[i][:, :], sAB[1][:, :], ALU.add))
                    fns.append(lambda e, c=c, i=i: e.tensor_scalar(
                            dst[c][:, :], sT2[i][:, :],
                            sNgC[:, c:c + 1], sNbC[:, c:c + 1],
                            ALU.mult, ALU.add))
                else:
                    fns.append(lambda e, c=c, i=i, d=dst: e.tensor_tensor(
                            d[c][:, :], sT2[i][:, :], sAB[1][:, :], ALU.add))
                P.grp(eng, fns,
                      reads=[rSrc[c], rAB[0], rAB[1], rConst],
                      writes=[rT2[i], rDst[c]])

        def ffn(post_evac=None, pre_slots=None):
            def unitsA(kb):
                ys = []
                zs = []
                head = kb // 2
                for yi in range(4):
                    oc0 = (kb % 2) * 16 + yi * 4
                    c_in = oc0 // 4
                    yslot = head * 8 + c_in
                    yb, ryb = (pY, rPY) if yi % 2 == 0 else (pS, rPS)

                    def em_y(yslot=yslot, c_in=c_in, yq=yi, yb=yb, ryb=ryb):
                        P.op("pe", lambda e: e.matmul(
                                yb[:, :], sBD[:, yslot * 128:(yslot + 1) * 128],
                                sXn[c_in][:, :], start=True, stop=True),
                             reads=[rXn[c_in], rConst], writes=[ryb])
                        P.op("act", lambda e: e.activation(
                                sY[yq][:, :], yb[:, :], AF.Copy),
                             reads=[ryb], writes=[rY[yq]])
                    ys.append(em_y)
                    for s in range(4):
                        oc = oc0 + s
                        w = (oc % 4) // 2
                        pb = 64 * w
                        cs = head * 16 + (oc // 4) * 2 + (oc % 2)
                        hslot = oc - (kb % 2) * 16
                        zb, rzb = (pZ, rPZ) if (4 * yi + s) % 2 == 0 else (pW, rPW)

                        def em_z(pb=pb, cs=cs, w=w, yq=yi, hslot=hslot, zb=zb,
                                 rzb=rzb):
                            P.op("pe", lambda e: e.matmul(
                                    zb[:, :],
                                    sZl[pb:pb + 64, cs * 128:(cs + 1) * 128],
                                    sY[yq][64 * w:64 * w + 64, :],
                                    start=True, stop=True),
                                 reads=[rY[yq], rConst], writes=[rzb])
                            P.op("act", lambda e: e.activation(
                                    sH[hslot][:, :], zb[:, :], AF.Gelu),
                                 reads=[rzb], writes=[rH[hslot]])
                        zs.append(em_z)
                us = [ys[0], ys[1]]
                for yi in range(4):
                    us.extend(zs[yi * 4:(yi + 1) * 4])
                    if yi + 2 < 4:
                        us.append(ys[yi + 2])
                return us

            for u in unitsA(0):
                u()
            for kb in range(8):
                nxt = unitsA(kb + 1) if kb < 7 else []
                ni = 0
                for fh in range(2):
                    for kk in range(16):
                        kg = kb * 16 + kk
                        pidx = fh * 16 + kk
                        if pre_slots is not None and kb == 0 and pidx < len(pre_slots):
                            slot = pre_slots[pidx]
                        else:
                            slot = ring_dma(AP(wc_bf, kg * 128 * 1024 + fh * 512,
                                               [[1024, 128], [1, 512]]),
                                            rWcD[kg // 8])
                        if kk == 0:
                            for fc in range(4):
                                P.op("pe", lambda e, slot=slot, fc=fc:
                                     e.matmul(pA[fc][:, :],
                                              sWc[slot][:, fc * 128:(fc + 1) * 128],
                                              sH[0][:, :],
                                              start=True, stop=False),
                                     reads=[rWc[slot], rH[0]], writes=[rPA[fc]])
                        else:
                            fns = []
                            for fc in range(4):
                                fns.append(lambda e, slot=slot, fc=fc, kk=kk:
                                           e.matmul(pA[fc][:, :],
                                                    sWc[slot][:, fc * 128:(fc + 1) * 128],
                                                    sH[kk][:, :],
                                                    start=False, stop=(kk == 15)))
                            P.grp("pe", fns,
                                  reads=[rWc[slot], rH[kk]],
                                  writes=([rPA[i] for i in range(4)]
                                          if kk == 15 else []))
                        if fh == 1:
                            want = (kk + 1) * len(nxt) // 16
                            while ni < want:
                                nxt[ni]()
                                ni += 1
                    for fc in range(4):
                        f = fh * 4 + fc
                        if kb == 0:
                            if fc < 2:
                                P.op("dve", lambda e, f=f, fc=fc: e.tensor_copy(
                                        sAcc[f][:, :], pA[fc][:, :]),
                                     reads=[rPA[fc]], writes=[rAcc[f]])
                            else:
                                P.op("act", lambda e, f=f, fc=fc: e.activation(
                                        sAcc[f][:, :], pA[fc][:, :], AF.Copy),
                                     reads=[rPA[fc]], writes=[rAcc[f]])
                        else:
                            P.op("dve", lambda e, f=f, fc=fc: e.tensor_tensor(
                                    sAcc[f][:, :], sAcc[f][:, :], pA[fc][:, :],
                                    ALU.add),
                                 reads=[rPA[fc]], writes=[rAcc[f]])
                        if kb == 7 and post_evac is not None:
                            post_evac(f)

        def x_slice(dram, t0, c, width=TT):
            return AP(dram, (c * 128) * TOK + t0, [[TOK, 128], [1, width]])

        ring_idx = [0]
        def warm(n):
            for _ in range(n):
                P.op("pe", lambda e: e.matmul(pW[:, 0:128], sOneRB[:, :],
                                              sOneRB[:, :], start=True, stop=True),
                     reads=[rConst], writes=[rPW])

        def ring_dma(src_ap, rdep):
            slot = ring_idx[0] % NWC
            ring_idx[0] += 1
            P.op("sp", lambda e, slot=slot, s=src_ap: e.dma_start(
                    out=sWc[slot][:, :], in_=s),
                 reads=[rdep], writes=[rWc[slot]], inc=16, asyn=True)
            return slot

        def load_x(ti):
            t0 = ti * TT
            for c in range(DC):
                P.op("act", lambda e, c=c, t0=t0: e.dma_start(
                        out=sX[c][:, :], in_=x_slice(x_in, t0, c)),
                     writes=[rX[c]], inc=16, asyn=True)

        load_x(0)
        for ti in range(NTILE):
            t0 = ti * TT
            if stage <= 2 and ti > 0:
                load_x(ti)
            if ti == 0 or stage <= 2:
                layernorm(sX, rX, fp32_bcast=False, affine=False,
                          dst_bf=sXn, rDst_bf=rXn)
            def x1_add(c):
                P.op("dve", lambda e, c=c: e.scalar_tensor_tensor(
                        sX[c][:, :], sAcc[c][:, :], 0.5, sX[c][:, :],
                        ALU.mult, ALU.add),
                     reads=[rAcc[c]], writes=[rX[c]])
            ffn(post_evac=x1_add, pre_slots=pre1 if ti > 0 else None)
            if stage == 1:
                for c in range(DC):
                    P.op("act", lambda e, c=c, t0=t0: e.dma_start(
                            out=x_slice(y_out, t0, c), in_=sX[c][:, :]),
                         reads=[rX[c]], sem="d_out", inc=16, asyn=True)
                continue
            layernorm(sX, rX, fp32_bcast=True, affine=False,
                      dst_f32=sAcc, rDst_f32=rAcc)
            layernorm(sAcc, rAcc, fp32_bcast=False, affine=True,
                      dst_bf=sXn, rDst_bf=rXn)
            for c in range(DC):
                P.op("dve", lambda e, c=c: e.tensor_tensor(
                        sAcc[c][:, :], sAcc[c][:, :], sX[c][:, :], ALU.add),
                     reads=[rX[c]], writes=[rAcc[c]])
                P.op("pool", lambda e, c=c, t0=t0: e.dma_start(
                        out=x_slice(xb_sp, t0, c), in_=sAcc[c][:, :]),
                     reads=[rAcc[c]], writes=[rXbD[ti]], inc=16, asyn=True)
            if stage >= 3 and ti + 1 < NTILE:
                load_x(ti + 1)
            if stage == 2:
                for c in range(DC):
                    P.op("act", lambda e, c=c, t0=t0: e.dma_start(
                            out=x_slice(y_out, t0, c), in_=sAcc[c][:, :]),
                         reads=[rAcc[c]], sem="d_out", inc=16, asyn=True)
                continue
            quadA = ([pA[0], pA[1], pA[2], pA[3]], [rPA[0], rPA[1], rPA[2], rPA[3]])
            quadB = ([pY, pS, pZ, pW], [rPY, rPS, rPZ, rPW])
            for fc4 in range(4):
                bks, rbks = quadA if fc4 % 2 == 0 else quadB
                for K in range(DC):
                    slot = ring_dma(AP(wh_bf, K * 128 * 4096 + fc4 * 512,
                                       [[4096, 128], [1, 512]]), rWhD)
                    if K == 0:
                        for tb in range(4):
                            P.op("pe", lambda e, tb=tb, slot=slot, bks=bks:
                                 e.matmul(bks[tb][:, :],
                                          sXn[0][:, tb * 128:(tb + 1) * 128],
                                          sWc[slot][:, :],
                                          start=True, stop=False),
                                 reads=[rWc[slot], rXn[0]], writes=[rbks[tb]])
                        for tb in range(4):
                            P.op("pe", lambda e, fc4=fc4, tb=tb, bks=bks:
                                 e.matmul(bks[tb][:, :], sOneRB[:, :],
                                          bh_row[0:1, fc4 * 512:(fc4 + 1) * 512],
                                          start=False, stop=False),
                                 reads=[rConst], writes=[rbks[tb]])
                    else:
                        P.grp("pe",
                              [lambda e, tb=tb, K=K, slot=slot, bks=bks: e.matmul(
                                      bks[tb][:, :],
                                      sXn[K][:, tb * 128:(tb + 1) * 128],
                                      sWc[slot][:, :],
                                      start=False, stop=(K == DC - 1))
                               for tb in range(4)],
                              reads=[rWc[slot], rXn[K]],
                              writes=(rbks if K == DC - 1 else []))
                for tb in range(4):
                    P.op("act", lambda e, fc4=fc4, tb=tb, bks=bks: e.activation(
                            sV[tb][:, fc4 * 512:(fc4 + 1) * 512], bks[tb][:, :],
                            AF.Silu),
                         reads=[rbks[tb]], writes=[rV[tb]])
            for g4 in range(4):
                bks, rbks = quadA if g4 % 2 == 0 else quadB
                for K in range(DC):
                    slot = ring_dma(AP(wh_bf, K * 128 * 4096 + 2048 + g4 * 512,
                                       [[4096, 128], [1, 512]]), rWhD)
                    P.grp("pe",
                          [lambda e, g4=g4, K=K, slot=slot, fc=fc, bks=bks:
                           e.matmul(bks[fc][:, :],
                                    sWc[slot][:, fc * 128:(fc + 1) * 128],
                                    sXn[K][:, :],
                                    start=(K == 0), stop=(K == DC - 1))
                           for fc in range(4)],
                          reads=[rWc[slot], rXn[K]],
                          writes=(rbks if K in (0, DC - 1) else []))
                for fc in range(4):
                    fcg = g4 * 4 + fc
                    P.op("act", lambda e, fcg=fcg, fc=fc, bks=bks: e.activation(
                            sGst[fcg % 4][:, :], bks[fc][:, :], AF.Silu,
                            bias=sBhC[:, 16 + fcg:17 + fcg], scale=np.float32(1.0)),
                         reads=[rbks[fc], rConst], writes=[rGst[fcg % 4]])
                    P.op("pool", lambda e, fcg=fcg, t0=t0: e.dma_start(
                            out=AP(gate_sp, fcg * 128 * TOK + t0,
                                   [[TOK, 128], [1, TT]]),
                            in_=sGst[fcg % 4][:, :]),
                         reads=[rGst[fcg % 4]], writes=[rGateD[ti]], inc=16, asyn=True)
            fns = []
            for K in range(DC):
                fns.append(lambda e, K=K: e.matmul(
                        pZ[:, :], sWqk[:, K * 128:(K + 1) * 128], sXn[K][:, :],
                        start=(K == 0), stop=(K == DC - 1)))
            P.grp("pe", fns, reads=[rConst] + [rXn[K] for K in range(DC)],
                  writes=[rPZ])
            P.op("act", lambda e: e.activation(sQs[:, :], pZ[:, :], AF.Silu,
                                               bias=sBqk[:, :],
                                               scale=np.float32(1.0)),
                 reads=[rPZ, rConst], writes=[rQs])
            for h in range(4):
                ga = sGB0[:, 0:1] if h == 0 else sGamT[:, h:h + 1]
                be = sGB0[:, 1:2] if h == 0 else sBetT[:, h:h + 1]
                dst = sHd3 if h == 3 else sHd[h]
                P.op("dve", lambda e, h=h, ga=ga, be=be, dst=dst: e.tensor_scalar(
                        dst[:, :], sQs[:, :], ga, be, ALU.mult, ALU.add),
                     reads=[rQs, rConst], writes=[rHd[h]])
            P.op("pool", lambda e, t0=t0: e.dma_start(
                    out=AP(lq_sp, t0, [[TOK, 128], [1, TT]]), in_=sHd[1][:, :]),
                 reads=[rHd[1]], writes=[rLqD[ti]], inc=16, asyn=True)
            for tb in range(4):
                P.op("pe", lambda e, tb=tb: e.transpose(
                        pW[:, tb * 128:(tb + 1) * 128],
                        sHd3[:, tb * 128:(tb + 1) * 128], sIdentF[:, :]),
                     reads=[rHd[3], rConst], writes=[rPW])
            P.op("act", lambda e: e.activation(sLkT[:, :], pW[:, 0:512], AF.Copy),
                 reads=[rPW], writes=[rLkT])
            if ti + 1 < NTILE:
                layernorm(sX, rX, fp32_bcast=False, affine=False,
                          dst_bf=sXn, rDst_bf=rXn)
                pre1 = [ring_dma(AP(wc_bf, kk * 128 * 1024, [[1024, 128], [1, 512]]),
                                 rWcD[0]) for kk in range(8)]
            else:
                pre1 = None
            for g in range(2):
                for jc in range(2):
                    pj = pZ if jc == 0 else pW
                    rpj = rPZ if jc == 0 else rPW
                    P.op("pe", lambda e, g=g, jc=jc, pj=pj: e.matmul(
                            pj[:, g * 256:(g + 1) * 256],
                            sHd[2][:, g * 256 + jc * 128:g * 256 + (jc + 1) * 128],
                            sHd[0][:, g * 256:(g + 1) * 256],
                            start=True, stop=True),
                         reads=[rHd[0], rHd[2]], writes=[rpj])
                    P.grp("dve",
                          [lambda e, g=g, jc=jc, pj=pj: e.scalar_tensor_tensor(
                                  sT2[jc][:, g * 256:(g + 1) * 256], sBias[jc][:, :],
                                  float(math.sqrt(128.0)),
                                  pj[:, g * 256:(g + 1) * 256],
                                  ALU.mult, ALU.add),
                           lambda e, g=g, jc=jc: e.scalar_tensor_tensor(
                                  sAtn[g * 2 + jc][:, :],
                                  sT2[jc][:, g * 256:(g + 1) * 256], 0.0,
                                  sT2[jc][:, g * 256:(g + 1) * 256],
                                  ALU.max, ALU.mult)],
                          reads=[rpj, rConst], writes=[rT2[jc], rAtn[g * 2 + jc]])
            for ec in range(16):
                bank = ec % 4
                fns = []
                for g in range(2):
                    for jc in range(2):
                        fns.append(lambda e, g=g, jc=jc, ec=ec, bank=bank: e.matmul(
                                pA[bank][:, g * 256:(g + 1) * 256],
                                sV[g * 2 + jc][:, ec * 128:(ec + 1) * 128],
                                sAtn[g * 2 + jc][:, :],
                                start=(jc == 0), stop=(jc == 1)))
                P.grp("pe", fns,
                      reads=[rV[0], rV[1], rV[2], rV[3]] + [rAtn[i] for i in range(4)],
                      writes=[rPA[bank]])
                P.op("act", lambda e, ec=ec, bank=bank: e.activation(
                        sQst[ec % 4][:, :], pA[bank][:, :], AF.Copy),
                     reads=[rPA[bank]], writes=[rQst[ec % 4]])
                P.op("pool", lambda e, ec=ec, t0=t0: e.dma_start(
                        out=AP(qo_sp, ec * 128 * TOK + t0,
                               [[TOK, 128], [1, TT]]),
                        in_=sQst[ec % 4][:, :]),
                     reads=[rQst[ec % 4]], writes=[rQoD[ti]],
                     inc=16, asyn=True)
            for ne in range(4):
                P.grp("pe",
                      [lambda e, ne=ne, tb=tb: e.matmul(
                              pA[ne][:, :],
                              sLkT[:, tb * 128:(tb + 1) * 128],
                              sV[tb][:, ne * 512:(ne + 1) * 512],
                              start=(tb == 0), stop=(tb == 3))
                       for tb in range(4)],
                      reads=[rLkT] + [rV[tb] for tb in range(4)],
                      writes=[rPA[ne]])
                if ti == 0:
                    P.op("dve", lambda e, ne=ne: e.tensor_copy(
                            sKv[:, ne * 512:(ne + 1) * 512], pA[ne][:, :]),
                         reads=[rPA[ne]], writes=[rKv])
                else:
                    P.op("dve", lambda e, ne=ne: e.tensor_tensor(
                            sKv[:, ne * 512:(ne + 1) * 512],
                            sKv[:, ne * 512:(ne + 1) * 512], pA[ne][:, :], ALU.add),
                         reads=[rPA[ne]], writes=[rKv])

        if stage >= 3:
            P.op("sp", lambda e: e.dma_start(out=cc_in.ap(), in_=sKv[:, :]),
                 reads=[rKv], writes=[rCc], inc=16, asyn=True)
            P.op("pool", lambda e: e.collective_compute(
                    "AllReduce", ALU.add,
                    replica_groups=[[0, 1], [2, 3], [4, 5], [6, 7]],
                    ins=[cc_in.ap()], outs=[cc_out.ap()]),
                 reads=[rCc], writes=[rCc], inc=1, asyn=True)
            P.op("sp", lambda e: e.dma_start(out=sKv[:, :], in_=cc_out.ap()),
                 reads=[rCc], writes=[rKv], inc=16, asyn=True)
            warm(800)
            P.op("act", lambda e: e.activation(sKvB[:, :], sKv[:, :], AF.Copy,
                                               scale=np.float32(1.0 / SEQ)),
                 reads=[rKv], writes=[rKvB])
            if tap_kv is not None:
                P.op("act", lambda e: e.dma_start(out=tap_kv.ap(), in_=sKv[:, :]),
                     reads=[rKv], sem="d_out", inc=16, asyn=True)

        if stage >= 4:
            def reload_xb_lq(ti):
                t0 = ti * TT
                for c in range(DC):
                    P.op("pool", lambda e, c=c, t0=t0: e.dma_start(
                            out=sX[c][:, :], in_=x_slice(xb_sp, t0, c)),
                         reads=[rXbD[ti]], writes=[rX[c]], inc=16, asyn=True)
                P.op("pool", lambda e, t0=t0: e.dma_start(
                        out=sHd[0][:, :],
                        in_=AP(lq_sp, t0, [[TOK, 128], [1, TT]])),
                     reads=[rLqD[ti]], writes=[rHd[0]], inc=16, asyn=True)

            def reload_qg(ti, ec):
                t0 = ti * TT
                P.op("pool", lambda e, ec=ec, t0=t0: e.dma_start(
                        out=sQo[ec % 4][:, :],
                        in_=AP(qo_sp, ec * 128 * TOK + t0,
                               [[TOK, 128], [1, TT]])),
                     reads=[rQoD[ti]], writes=[rQo[ec % 4]], inc=16, asyn=True)
                P.op("pool", lambda e, ec=ec, t0=t0: e.dma_start(
                        out=sGst[ec % 4][:, :],
                        in_=AP(gate_sp, ec * 128 * TOK + t0,
                               [[TOK, 128], [1, TT]])),
                     reads=[rGateD[ti]], writes=[rGst[ec % 4]],
                     inc=16, asyn=True)

            for ec in range(4):
                reload_qg(0, ec)
            for ti in range(NTILE):
                t0 = ti * TT
                reload_xb_lq(ti)
                for ec in range(16):
                    pj = pY if ec % 2 == 0 else pZ
                    rpj = rPY if ec % 2 == 0 else rPZ
                    P.op("pe", lambda e, ec=ec, pj=pj: e.matmul(
                            pj[:, :], sKvB[:, ec * 128:(ec + 1) * 128],
                            sHd[0][:, :], start=True, stop=True),
                         reads=[rKvB, rHd[0]], writes=[rpj])
                    P.grp("dve",
                          [lambda e, ec=ec, pj=pj: e.tensor_tensor(
                                  sTb[ec % 2][:, :], sQo[ec % 4][:, :], pj[:, :],
                                  ALU.add),
                           lambda e, ec=ec: e.tensor_tensor(
                                  sH[ec][:, :], sTb[ec % 2][:, :],
                                  sGst[ec % 4][:, :], ALU.mult)],
                          reads=[rpj, rGst[ec % 4], rQo[ec % 4]],
                          writes=[rTb[ec % 2], rH[ec]])
                    if ec + 4 < 16:
                        reload_qg(ti, ec + 4)
                    elif ti + 1 < NTILE:
                        reload_qg(ti + 1, ec - 12)
                banks = [pA[0], pA[1], pA[2], pA[3], pY, pZ, pS, pW]
                rbanks = [rPA[0], rPA[1], rPA[2], rPA[3], rPY, rPZ, rPS, rPW]
                for ec in range(16):
                    for half in range(2):
                        slot = ring_dma(AP(wo_bf, ec * 128 * 1024 + half * 512,
                                           [[1024, 128], [1, 512]]), rWoD)
                        P.grp("pe",
                              [lambda e, ec=ec, half=half, slot=slot, fc=fc:
                               e.matmul(banks[half * 4 + fc][:, :],
                                        sWc[slot][:, fc * 128:(fc + 1) * 128],
                                        sH[ec][:, :],
                                        start=(ec == 0), stop=False)
                               for fc in range(4)],
                              reads=[rWc[slot], rH[ec]],
                              writes=([rbanks[half * 4 + i] for i in range(4)]
                                      if ec == 0 else []))
                P.grp("pe",
                      [lambda e, f=f: e.matmul(
                              banks[f][:, :],
                              sRowB2[0:1, f * 128:(f + 1) * 128],
                              sOne512B[:, :], start=False, stop=True)
                       for f in range(8)],
                      reads=[rConst], writes=rbanks)
                for f in range(8):
                    P.op("dve", lambda e, f=f: e.tensor_tensor(
                            sX[f][:, :], sX[f][:, :], banks[f][:, :], ALU.add),
                         reads=[rbanks[f]], writes=[rX[f]])
                pre2 = [ring_dma(AP(wc_bf, kk * 128 * 1024,
                                     [[1024, 128], [1, 512]]), rWcD[0])
                        for kk in range(8)]
                layernorm(sX, rX, fp32_bcast=False, affine=False,
                          dst_bf=sXn, rDst_bf=rXn)
                def fin_add(c, t0=t0):
                    P.op("dve", lambda e, c=c: e.scalar_tensor_tensor(
                            sX[c][:, :], sAcc[c][:, :], 0.5, sX[c][:, :],
                            ALU.mult, ALU.add),
                         reads=[rAcc[c]], writes=[rX[c]])
                    P.op("act", lambda e, c=c, t0=t0: e.dma_start(
                            out=x_slice(y_out, t0, c), in_=sX[c][:, :]),
                         reads=[rX[c]], sem="d_out", inc=16, asyn=True)
                ffn(post_evac=fin_add, pre_slots=pre2)

        out_total = P.cnt["d_out"]
        if out_total:
            P.q["sp"].append(lambda e, v=out_total, s=P.sems["d_out"]:
                             e.wait_ge(s, v))

        with nc.Block() as block:
            @block.tensor
            def _(t):
                for f in P.q["pe"]:
                    f(t)

            @block.scalar
            def _(s):
                for f in P.q["act"]:
                    f(s)

            @block.vector
            def _(v):
                for f in P.q["dve"]:
                    f(v)

            @block.gpsimd
            def _(g):
                for f in P.q["pool"]:
                    f(g)

            @block.sync
            def _(sy):
                for f in P.q["sp"]:
                    f(sy)

    return nc


def _bucket_table():
    GSZ, NB, MAXD = 256, 32, 128
    pos = np.arange(GSZ)
    n = pos[:, None] - pos[None, :]
    nbh = NB // 2
    ret = (n < 0).astype(np.int32) * nbh
    na = np.abs(n)
    max_exact = nbh // 2
    is_small = na < max_exact
    val_large = max_exact + (
        np.log(np.maximum(na, 1).astype(np.float32) / max_exact)
        / math.log(MAXD / max_exact) * (nbh - max_exact)).astype(np.int32)
    val_large = np.minimum(val_large, nbh - 1)
    return ret + np.where(is_small, na, val_large)


def _host_prep(inputs):
    x = np.asarray(inputs["L0_attn_x"], dtype=np.float32)
    rail_maker = np.asarray(inputs["L0_ffn_rail_maker"], np.float32)
    rail_up = np.asarray(inputs["L0_ffn_rail_up_proj"], np.float32)
    w_cproj = np.asarray(inputs["L0_ffn_w_cproj"], np.float32)
    W_h = np.asarray(inputs["L0_attn_W_h"], np.float32)
    W_qk = np.asarray(inputs["L0_attn_W_qk"], np.float32)
    W_o = np.asarray(inputs["L0_attn_W_o"], np.float32)
    b_h = np.asarray(inputs["L0_attn_b_h"], np.float32)
    b_qk = np.asarray(inputs["L0_attn_b_qk"], np.float32)
    b_o = np.asarray(inputs["L0_attn_b_o"], np.float32)
    qk_gamma = np.asarray(inputs["L0_attn_qk_gamma"], np.float32)
    qk_beta = np.asarray(inputs["L0_attn_qk_beta"], np.float32)
    norm_g = np.asarray(inputs["L0_attn_norm_g"], np.float32)
    norm_b = np.asarray(inputs["L0_attn_norm_b"], np.float32)
    rel_emb = np.asarray(inputs["L0_attn_rel_emb"], np.float32)

    xf = x.reshape(32768, 1024)
    shards = [np.ascontiguousarray(xf[c * TOK:(c + 1) * TOK].T)
              for c in range(NCORES)]
    wcT = np.ascontiguousarray(w_cproj.T)

    BD = np.zeros((128, 4096), np.float32)
    for h in range(4):
        for c in range(8):
            slot = h * 8 + c
            for rl in range(8):
                BD[rl * 16:(rl + 1) * 16,
                   slot * 128 + rl * 16: slot * 128 + (rl + 1) * 16] = \
                    rail_maker[h, 8 * c + rl]
    ZL = np.zeros((128, 8192), np.float32)
    for h in range(4):
        for oc in range(32):
            w = (oc % 4) // 2
            pb = 64 * w
            cs = h * 16 + (oc // 4) * 2 + (oc % 2)
            rr0 = 2 * (oc % 4) - 4 * w
            for rl2 in range(2):
                rail = 2 * oc + rl2
                rr = rr0 + rl2
                ZL[pb + rr * 16: pb + (rr + 1) * 16,
                   cs * 128 + rl2 * 64: cs * 128 + (rl2 + 1) * 64] = \
                    rail_up[h, rail].T

    bucketT = _bucket_table().T
    biasr = np.ascontiguousarray(rel_emb[bucketT, 0].astype(np.float32))

    common = dict(
        wcT=wcT, w_h=W_h, w_o=W_o, w_qk=W_qk, bd=BD, zl=ZL,
        b_h_row=np.ascontiguousarray(b_h.reshape(1, 4096)),
        b_h_col=np.ascontiguousarray(b_h.reshape(32, 128).T),
        b_o_row=np.ascontiguousarray(b_o.reshape(1, 1024)),
        b_qk_col=np.ascontiguousarray(b_qk.reshape(128, 1)),
        gam_t=np.ascontiguousarray(qk_gamma.T),
        bet_t=np.ascontiguousarray(qk_beta.T),
        ng_col=np.ascontiguousarray(norm_g.reshape(8, 128).T),
        nb_col=np.ascontiguousarray(norm_b.reshape(8, 128).T),
        biasr=biasr,
        onesc_b=np.ones((128, 1), bf16), onesr_b=np.ones((1, 128), bf16),
        onesr_f=np.ones((1, 128), np.float32),
        ones512_b=np.ones((1, 512), bf16),
        ones512_f=np.ones((1, 512), np.float32),
        ident_b=np.eye(128, dtype=bf16),
        ident_f=np.eye(128, dtype=np.float32),
    )
    return [dict(common, x_fm=shards[c]) for c in range(NCORES)]


_NC_CACHE = {}


def kernel(**inputs):
    in_maps = _host_prep(inputs)
    key = (STAGE, TAP_KV)
    if key not in _NC_CACHE:
        _NC_CACHE[key] = build_program(STAGE, TAP_KV)
    nc = _NC_CACHE[key]
    res = run_bass_kernel_spmd(nc, in_maps, core_ids=list(range(NCORES)))
    outs = [r["y_fm"] for r in res.results]
    full = np.concatenate([o.T for o in outs], axis=0)
    return np.ascontiguousarray(full.reshape(4, 8192, 1024), dtype=np.float32)
